# version 15
# baseline (speedup 1.0000x reference)
import numpy as np
from contextlib import ExitStack
import concourse.bass as bass
import concourse.mybir as mybir
from concourse.bass_utils import run_bass_kernel_spmd

F32 = mybir.dt.float32
BF16 = mybir.dt.bfloat16
AF = mybir.ActivationFunctionType
ALU = mybir.AluOpType
AX = mybir.AxisListType

D = 2048
T = 2048
CH = 16
NQT = 16
SPAN = 512
NSPAN = 4
H = 16
E = 8
R = 64
ALPHA = float(1.0 / np.sqrt(128.0))
EPS = 1e-5
MASKNEG = -1.0e30
INV_D = float(1.0 / D)
INV_RANK = float(1.0 / R)

ENGINES = ("sync", "tensor", "vector", "scalar", "gpsimd")


class _Sem:
    def __init__(self, name, handle):
        self.name = name
        self.h = handle
        self.n = 0


class _Ctx:
    def __init__(self, nc, sems):
        self.nc = nc
        self.streams = {e: [] for e in ENGINES}
        self.sems = sems
        self._lastw = {}
        self.strict = False

    def wait(self, eng, sem, val):
        if val is None or val <= 0:
            return
        key = (eng, sem.name)
        if self._lastw.get(key, 0) >= val:
            return
        self._lastw[key] = val
        self.streams[eng].append(lambda Eng, h=sem.h, v=val: Eng.wait_ge(h, v))

    def ins(self, eng, fn, sem=None, k=1, waits=()):
        if eng == "vector":
            self._vn = getattr(self, "_vn", 0) + 1
            import sys as _sys
            f = _sys._getframe(1)
            if not hasattr(self, "vlog"):
                self.vlog = {}
            self.vlog[self._vn] = f.f_lineno
        do_strict = (self.strict is True or
                     (isinstance(self.strict, (set, frozenset)) and eng in self.strict) or
                     (isinstance(self.strict, tuple) and eng == "vector"
                      and self.strict[0] <= self._vn < self.strict[1]))
        if do_strict:
            for s in self.sems.values():
                self.wait(eng, s, s.n)
        for (s, v) in waits:
            self.wait(eng, s, v)
        if sem is None:
            self.streams[eng].append(fn)
            return None
        sem.n += k
        v = sem.n
        self.streams[eng].append(lambda Eng, f=fn, h=sem.h, kk=k: f(Eng).then_inc(h, kk))
        return v

    def dma(self, eng, out_ap, in_ap, sem, waits=(), slow=False):
        if slow:
            return self.ins(eng, lambda Eng, o=out_ap, i=in_ap: Eng.dma_start(
                out=o, in_=i, allow_slow_non_contiguous=True), sem=sem, k=16, waits=waits)
        return self.ins(eng, lambda Eng, o=out_ap, i=in_ap: Eng.dma_start(out=o, in_=i),
                        sem=sem, k=16, waits=waits)

    def barrier(self):
        snap = {name: s.n for name, s in self.sems.items()}
        for eng in ENGINES:
            for name, v in snap.items():
                self.wait(eng, self.sems[name], v)


class _StopBuild(Exception):
    pass


def build_nc(debug_outputs=(), phases=4, strict=False):
    nc = bass.Bass("TRN2", target_bir_lowering=False)

    xT = nc.dram_tensor("xT", [D, T], F32, kind="ExternalInput").ap()
    wqT = nc.dram_tensor("wqT", [D, D], F32, kind="ExternalInput").ap()
    wkT = nc.dram_tensor("wkT", [D, D], F32, kind="ExternalInput").ap()
    wvT = nc.dram_tensor("wvT", [D, D], F32, kind="ExternalInput").ap()
    woT = nc.dram_tensor("woT", [D, D], F32, kind="ExternalInput").ap()
    aT = nc.dram_tensor("aT", [D, E * R], F32, kind="ExternalInput").ap()
    bS = nc.dram_tensor("bS", [E * R, D], F32, kind="ExternalInput").ap()
    wrT = nc.dram_tensor("wrT", [D, E], F32, kind="ExternalInput").ap()
    br = nc.dram_tensor("br", [1, E], F32, kind="ExternalInput").ap()
    outT = nc.dram_tensor("outT", [D, T], F32, kind="ExternalOutput").ap()

    def scratch(name, dtype=F32):
        kind = "ExternalOutput" if name in debug_outputs else "Internal"
        return nc.dram_tensor(name + "_s", [D, T], dtype, kind=kind).ap()

    qTsh = scratch("qh", BF16)
    qTsl = scratch("ql", BF16)
    kTsh = scratch("kh", BF16)
    kTsl = scratch("kl", BF16)
    vSh = scratch("vh", BF16)
    vSl = scratch("vl", BF16)
    attnSh = scratch("ah", BF16)
    attnSl = scratch("al", BF16)
    x1Ts = scratch("x1")
    nrow_s = nc.dram_tensor("nrow_s", [1, 512], F32, kind="Internal").ap()
    dbgP = nc.dram_tensor("dbgP_s", [3 * 128, T], F32,
                          kind="ExternalOutput" if ("p2dbg" in debug_outputs or "p4dbg" in debug_outputs) else "Internal").ap()
    dbgC = nc.dram_tensor("dbgC_s", [128, 16], F32,
                          kind="ExternalOutput" if ("p2dbg" in debug_outputs or "p4dbg" in debug_outputs) else "Internal").ap()

    es = ExitStack()
    sems = {}
    for name in ("LD", "ST", "PE", "DV", "SC", "GP"):
        sems[name] = _Sem(name, es.enter_context(nc.semaphore(name)))
    LD, ST, PE, DV, SC, GP = (sems[k] for k in ("LD", "ST", "PE", "DV", "SC", "GP"))
    ctx = _Ctx(nc, sems)
    ctx.strict = strict

    def sb(name, shape, stack=es, dtype=F32):
        return stack.enter_context(nc.sbuf_tensor(name, shape, dtype))

    def psum(name, shape, stack=es):
        return stack.enter_context(nc.psum_tensor(name, shape, F32))

    ident = sb("ident", [128, 128])
    cmaskT = sb("cmaskT", [128, 128])
    onesD = sb("onesD", [128, 128])
    ones1 = sb("ones1", [128, 128])
    onec = sb("onec", [1, 128])
    onescol = sb("onescol", [128, 1])
    one1 = sb("one1", [1, 1])
    br_bc = sb("br_bc", [128, E])
    br_row = sb("br_row", [1, E])
    ioA = sb("ioA", [128, 128])
    ioB = sb("ioB", [128, 128])

    ctx.ins("gpsimd", lambda Eg: Eg.iota(ioA[:, :], [[1, 128]], channel_multiplier=0,
                                         allow_small_or_imprecise_dtypes=True))
    ctx.ins("gpsimd", lambda Eg: Eg.iota(ioB[:, :], [[0, 128]], channel_multiplier=1,
                                         allow_small_or_imprecise_dtypes=True))
    ctx.ins("gpsimd", lambda Eg: Eg.iota(selE0 if False else ioA[:, :], [[1, 128]], channel_multiplier=0,
                                         allow_small_or_imprecise_dtypes=True), sem=GP)
    g_iota = GP.n
    ctx.ins("vector", lambda Ev: Ev.tensor_tensor(ident[:, :], ioA[:, :], ioB[:, :], ALU.is_equal),
            waits=((GP, g_iota),))
    ctx.ins("vector", lambda Ev: Ev.tensor_tensor(cmaskT[:, :], ioB[:, :], ioA[:, :], ALU.is_gt))
    ctx.ins("vector", lambda Ev: Ev.tensor_scalar_mul(cmaskT[:, :], cmaskT[:, :], MASKNEG))
    ctx.ins("vector", lambda Ev: Ev.memset(onesD[:, :], INV_D))
    ctx.ins("vector", lambda Ev: Ev.memset(ones1[:, :], 1.0))
    ctx.ins("vector", lambda Ev: Ev.memset(onec[:, :], 1.0))
    ctx.ins("vector", lambda Ev: Ev.memset(onescol[:, :], 1.0))
    ctx.ins("vector", lambda Ev: Ev.memset(one1[:, :], 1.0), sem=DV)
    g_consts = DV.n

    attn_stack = ExitStack()
    try:
      with ExitStack() as p1o:
          xhi = sb("xhi", [128, CH, T], p1o, BF16)
          xlo = sb("xlo", [128, CH, T], p1o, BF16)
          r1bc = sb("r1bc", [128, T], p1o)
          r1col = sb("r1col", [128, NQT], p1o)

          with ExitStack() as p1a:
              xst = sb("xst", [128, 2, T], p1a)
              xsq = sb("xsq", [128, 2, T], p1a)
              m1bc = sb("m1bc", [128, T], p1a)
              st_ps = psum("st_ps", [128, 8, SPAN], p1a)

              sq_done = [0, 0]
              last_s2 = 0
              last_s1 = [0, 0]
              for c in range(CH):
                  par = c % 2
                  xv = ctx.dma("sync", xst[:, par, :], xT[c * 128:(c + 1) * 128, :], LD,
                               waits=((PE, max(sq_done[par], last_s1[par])),))
                  for sp in range(NSPAN):
                      v = ctx.ins("tensor", lambda Et, c=c, sp=sp, par=par: Et.matmul(
                          st_ps[:, sp, :], onesD[:, :], xst[:, par, sp * SPAN:(sp + 1) * SPAN],
                          start=(c == 0), stop=(c == CH - 1)),
                          sem=(PE if sp == NSPAN - 1 else None),
                          waits=((LD, xv), (DV, g_consts)))
                  last_s1[par] = v
                  sqv = ctx.ins("scalar", lambda Es, c=c, par=par: Es.activation(
                      xsq[:, par, :], xst[:, par, :], AF.Square),
                      sem=SC, waits=((LD, xv), (PE, sq_done[par])))
                  for sp in range(NSPAN):
                      v = ctx.ins("tensor", lambda Et, par=par, sp=sp, c=c: Et.matmul(
                          st_ps[:, 4 + sp, :], onesD[:, :], xsq[:, par, sp * SPAN:(sp + 1) * SPAN],
                          start=(c == 0), stop=(c == CH - 1)),
                          sem=(PE if sp == NSPAN - 1 else None), waits=((SC, sqv),))
                      if sp == NSPAN - 1:
                          sq_done[par] = v
                          last_s2 = v

              for sp in range(NSPAN):
                  ctx.ins("vector", lambda Ev, sp=sp: Ev.tensor_copy(
                      m1bc[:, sp * SPAN:(sp + 1) * SPAN], st_ps[:, sp, :]),
                      waits=((PE, last_s2),))
                  ctx.ins("vector", lambda Ev, sp=sp: Ev.tensor_mul(
                      xsq[:, 0, sp * SPAN:(sp + 1) * SPAN],
                      m1bc[:, sp * SPAN:(sp + 1) * SPAN], m1bc[:, sp * SPAN:(sp + 1) * SPAN]))
                  ctx.ins("vector", lambda Ev, sp=sp: Ev.tensor_tensor(
                      r1bc[:, sp * SPAN:(sp + 1) * SPAN], st_ps[:, 4 + sp, :],
                      xsq[:, 0, sp * SPAN:(sp + 1) * SPAN], ALU.subtract), sem=DV)
              ctx.ins("vector", lambda Ev: Ev.tensor_scalar_add(r1bc[:, :], r1bc[:, :], EPS),
                      sem=DV)
              g_var = DV.n
              sqv = ctx.ins("scalar", lambda Es: Es.activation(
                  r1bc[:, :], r1bc[:, :], AF.Sqrt), sem=SC, waits=((DV, g_var),))
              ctx.ins("vector", lambda Ev: Ev.reciprocal(r1bc[:, :], r1bc[:, :]),
                      sem=DV, waits=((SC, sqv),))
              g_r1 = DV.n
              cp = g_r1
              for i in range(NQT):
                  v = ctx.ins("tensor", lambda Et, i=i: Et.matmul(
                      st_ps[:, 0, 0:1], r1bc[0:1, i * 128:(i + 1) * 128], one1[:, :],
                      start=True, stop=True), sem=PE, waits=((DV, cp),))
                  cp = ctx.ins("vector", lambda Ev, i=i: Ev.tensor_copy(
                      r1col[:, i:i + 1], st_ps[:, 0, 0:1]), sem=DV, waits=((PE, v),))
              sp_done = [g_r1, g_r1]
              for c in range(CH):
                  par = c % 2
                  xv = ctx.dma("sync", xst[:, par, :], xT[c * 128:(c + 1) * 128, :], LD,
                               waits=((DV, sp_done[par]),))
                  ctx.ins("vector", lambda Ev, c=c, par=par: Ev.tensor_tensor(
                      xst[:, par, :], xst[:, par, :], m1bc[:, :], ALU.subtract),
                      waits=((LD, xv),))
                  ctx.ins("vector", lambda Ev, c=c, par=par: Ev.tensor_copy(
                      xhi[:, c, :], xst[:, par, :]))
                  sp_done[par] = ctx.ins("vector", lambda Ev, c=c, par=par: Ev.tensor_tensor(
                      xlo[:, c, :], xst[:, par, :], xhi[:, c, :], ALU.subtract), sem=DV)
              ctx.barrier()

          with ExitStack() as p1b:
              wbuf = sb("wbuf", [128, 2, CH, 128], p1b)
              whi = sb("whi", [128, 2, CH, 128], p1b, BF16)
              wlo = sb("wlo", [128, 2, CH, 128], p1b, BF16)
              stage = sb("stage", [128, 2, SPAN], p1b)
              stageh = sb("stageh", [128, 2, T], p1b, BF16)
              stagel = sb("stagel", [128, 2, T], p1b, BF16)
              pp = psum("pp", [128, 2, SPAN], p1b)
              wb_free = [0, 0]
              wsp_done = [0, 0]
              st_free = [0, 0]
              bank_free = [0, 0]
              for wsrc, wdsth, wdstl in ((wqT, qTsh, qTsl), (wkT, kTsh, kTsl)):
                  for oc in range(CH):
                      par = oc % 2
                      wv = ctx.dma("sync", wbuf[:, par, :, :],
                                   wsrc[:, oc * 128:(oc + 1) * 128].rearrange("(c p) o -> p c o", p=128),
                                   LD, waits=((DV, wsp_done[par]),))
                      ctx.ins("vector", lambda Ev, par=par: Ev.tensor_copy(
                          whi[:, par, :, :], wbuf[:, par, :, :]),
                          waits=((LD, wv), (PE, wb_free[par])))
                      wsp_done[par] = ctx.ins("vector", lambda Ev, par=par: Ev.tensor_tensor(
                          wlo[:, par, :, :], wbuf[:, par, :, :], whi[:, par, :, :],
                          ALU.subtract), sem=DV)
                      evs = 0
                      for sp in range(NSPAN):
                          bank = sp % 2
                          for c in range(CH):
                              for ph, (wt, xt_) in enumerate(((whi, xhi), (whi, xlo), (wlo, xhi))):
                                  v = ctx.ins("tensor", lambda Et, par=par, c=c, sp=sp, bank=bank, wt=wt, xt_=xt_, first=(c == 0 and ph == 0), last=(c == CH - 1 and ph == 2): Et.matmul(
                                      pp[:, bank, :], wt[:, par, c, :],
                                      xt_[:, c, sp * SPAN:(sp + 1) * SPAN],
                                      start=first, stop=last),
                                      sem=(PE if (c == CH - 1 and ph == 2) else None),
                                      waits=((DV, max(wsp_done[par], bank_free[bank])),))
                          mmv = v
                          if sp == NSPAN - 1:
                              wb_free[par] = mmv
                          ctx.ins("vector", lambda Ev, par=par, sp=sp, bank=bank: Ev.tensor_tensor(
                              stage[:, par, :], pp[:, bank, :],
                              r1bc[:, sp * SPAN:(sp + 1) * SPAN], ALU.mult),
                              waits=((PE, mmv), (ST, st_free[par])))
                          ctx.ins("vector", lambda Ev, par=par, sp=sp: Ev.tensor_copy(
                              stageh[:, par, sp * SPAN:(sp + 1) * SPAN],
                              stage[:, par, :]))
                          bank_free[bank] = ctx.ins("vector", lambda Ev, par=par, sp=sp: Ev.tensor_tensor(
                              stagel[:, par, sp * SPAN:(sp + 1) * SPAN],
                              stage[:, par, :],
                              stageh[:, par, sp * SPAN:(sp + 1) * SPAN], ALU.subtract),
                              sem=DV)
                          evs = bank_free[bank]
                      ctx.dma("gpsimd", wdsth[oc * 128:(oc + 1) * 128, :],
                              stageh[:, par, :], ST, waits=((DV, evs),))
                      st_free[par] = ctx.dma("gpsimd", wdstl[oc * 128:(oc + 1) * 128, :],
                                             stagel[:, par, :], ST)
              ctx.barrier()

          with ExitStack() as p1c:
              VOS = 512
              vw = r1bc[:, :].rearrange("p (c o) -> p c o", o=128)
              vdum = sb("vdum", [128, 1], p1c)
              vwhi = sb("vwhi", [128, 2, CH, VOS], p1c, BF16)
              vwlo = sb("vwlo", [128, 2, CH, VOS], p1c, BF16)
              vstageh = sb("vstageh", [128, 2, SPAN], p1c, BF16)
              vstagel = sb("vstagel", [128, 2, SPAN], p1c, BF16)
              ppv = psum("ppv", [128, 2, SPAN], p1c)
              vsp_done = DV.n
              vsp_blk = [0, 0]
              blk_mm = [0, 0]
              vst_free = [0, 0]
              vbank_free = [0, 0]

              def vw_load(os_, q):
                  return ctx.dma("sync", vw,
                                 wvT[:, os_ * VOS + q * 128:os_ * VOS + (q + 1) * 128].rearrange("(c p) o -> p c o", p=128),
                                 LD, waits=((DV, vsp_done),))

              def vw_split(os_, q, wv):
                  nonlocal vsp_done
                  wpar = os_ % 2
                  ctx.ins("vector", lambda Ev, q=q, wpar=wpar: Ev.tensor_copy(
                      vwhi[:, wpar, :, q * 128:(q + 1) * 128], vw),
                      waits=((LD, wv), (PE, blk_mm[wpar])))
                  vsp_done = ctx.ins("vector", lambda Ev, q=q, wpar=wpar: Ev.tensor_tensor(
                      vwlo[:, wpar, :, q * 128:(q + 1) * 128], vw,
                      vwhi[:, wpar, :, q * 128:(q + 1) * 128], ALU.subtract), sem=DV)
                  if q == 3:
                      vsp_blk[wpar] = vsp_done

              for q in range(4):
                  _wv = vw_load(0, q)
                  vw_split(0, q, _wv)
              pend_wv = [0]
              for os_ in range(4):
                  wpar = os_ % 2
                  for tt in range(NQT):
                      if os_ + 1 < 4:
                          if tt == 1:
                              pend_wv[0] = vw_load(os_ + 1, 0)
                          elif tt in (3, 5, 7):
                              vw_split(os_ + 1, (tt - 3) // 2, pend_wv[0])
                              pend_wv[0] = vw_load(os_ + 1, (tt - 1) // 2)
                          elif tt == 9:
                              vw_split(os_ + 1, 3, pend_wv[0])
                      bank = tt % 2
                      for c in range(CH):
                          for ph, (xt_, wt) in enumerate(((xhi, vwhi), (xhi, vwlo), (xlo, vwhi))):
                              v = ctx.ins("tensor", lambda Et, c=c, tt=tt, bank=bank, wpar=wpar, xt_=xt_, wt=wt, first=(c == 0 and ph == 0), last=(c == CH - 1 and ph == 2): Et.matmul(
                                  ppv[:, bank, 0:VOS], xt_[:, c, tt * 128:(tt + 1) * 128], wt[:, wpar, c, :],
                                  start=first, stop=last),
                                  sem=(PE if (c == CH - 1 and ph == 2) else None),
                                  waits=((DV, max(vsp_blk[wpar], vbank_free[bank])),))
                      mmv = v
                      if tt == NQT - 1:
                          blk_mm[wpar] = mmv
                      spar = tt % 2
                      ctx.ins("vector", lambda Ev, spar=spar, bank=bank, tt=tt: Ev.tensor_scalar(
                          vstageh[:, spar, 0:VOS], ppv[:, bank, 0:VOS], r1col[:, tt:tt + 1], None, ALU.mult),
                          waits=((PE, mmv), (ST, vst_free[spar])))
                      ctx.ins("vector", lambda Ev: Ev.tensor_copy(vdum[:, :], vdum[:, :]))
                      vbank_free[bank] = ctx.ins("vector", lambda Ev, spar=spar, bank=bank, tt=tt: Ev.scalar_tensor_tensor(
                          vstagel[:, spar, 0:VOS], ppv[:, bank, 0:VOS], r1col[:, tt:tt + 1],
                          vstageh[:, spar, 0:VOS], ALU.mult, ALU.subtract), sem=DV)
                      ctx.dma("gpsimd", vSh[tt * 128:(tt + 1) * 128, os_ * VOS:(os_ + 1) * VOS],
                              vstageh[:, spar, 0:VOS], ST, waits=((DV, vbank_free[bank]),))
                      vst_free[spar] = ctx.dma(
                          "gpsimd", vSl[tt * 128:(tt + 1) * 128, os_ * VOS:(os_ + 1) * VOS],
                          vstagel[:, spar, 0:VOS], ST)
              p1_store_total = ST.n
              ctx.barrier()
      if phases <= 1:
          raise _StopBuild

      with ExitStack() as p2:
          qhh = sb("qhh", [128, 2, T], p2, BF16)
          qhl = sb("qhl", [128, 2, T], p2, BF16)
          khh = sb("khh", [128, 2, T], p2, BF16)
          khl = sb("khl", [128, 2, T], p2, BF16)
          vhh = sb("vhh", [128, 2, NQT, 128], p2, BF16)
          vhl = sb("vhl", [128, 2, NQT, 128], p2, BF16)
          NPB = 4
          NS = 4
          Pb = sb("Pb", [128, NPB, SPAN], p2)
          PhiT = sb("PhiT", [128, NS, SPAN], p2, BF16)
          PloT = sb("PloT", [128, NS, SPAN], p2, BF16)
          athb = sb("athb", [128, SPAN], p2, BF16)
          atlb = sb("atlb", [128, SPAN], p2, BF16)
          tmpn = sb("tmpn", [128, SPAN], p2)
          nrowL = sb("nrowL", [1, 2, SPAN], p2)
          nrow = sb("nrow", [1, 2, SPAN], p2)
          ninvbc = sb("ninvbc", [128, 2, SPAN], p2)
          dumm = sb("dumm", [128, 1], p2)
          S_ps = psum("S_ps", [128, 4 * SPAN], p2)
          PV_ps = psum("PV_ps", [128, SPAN], p2)
          chain_ps = psum("chain_ps", [128, SPAN], p2)
          lrow_ps = [psum("lrA", [1, SPAN], p2), psum("lrB", [1, SPAN], p2)]

          ctx.ins("vector", lambda Ev: Ev.memset(dumm[:, :], 0.0))
          ctx.ins("vector", lambda Ev: Ev.memset(ninvbc[:, :, :], 0.0), sem=DV)

          s_exp_rel = [0, 0, 0, 0]
          pb_rel_dv = [0] * NPB
          pb_rel_pe = [0] * NPB
          ps_rel = [0] * NS
          pv_free = 0
          at_st = 0
          nbc_rel = [0, 0]
          nbc_war = [0, 0]
          lr_war = [0, 0]
          nrow_war = [0, 0]
          chain_rel = 0
          hd_free = [0, 0]
          kv_free = [0, 0]
          q_ld = [0, 0]
          k_ld = [0, 0]
          v_ld = [0, 0]
          sck = 0
          pbk = 0
          psk = 0
          ck = 0
          lo_val = {}
          llast = [0, 0]
          pv_last = [0]
          exq = [None, None]
          bqueue = []
          prev_hi = [None]

          def drain_thru(tag):
              while bqueue and bqueue[0][0] <= tag:
                  bqueue.pop(0)[1]()

          for h in range(H):
              par = h % 2
              q_ld[par] = ctx.dma("sync", qhh[:, par, :], qTsh[h * 128:(h + 1) * 128, :], LD,
                                  waits=((PE, hd_free[par]), (ST, p1_store_total)))
              q_ld[par] = ctx.dma("sync", qhl[:, par, :], qTsl[h * 128:(h + 1) * 128, :], LD)
              ctx.wait("sync", PE, kv_free[par])
              k_ld[par] = ctx.dma("sync", khh[:, par, :], kTsh[h * 128:(h + 1) * 128, :], LD)
              k_ld[par] = ctx.dma("sync", khl[:, par, :], kTsl[h * 128:(h + 1) * 128, :], LD)
              v_ld[par] = ctx.dma("sync", vhh[:, par, :, :],
                             vSh[:, h * 128:(h + 1) * 128].rearrange("(c p) o -> p c o", p=128), LD)
              v_ld[par] = ctx.dma("sync", vhl[:, par, :, :],
                             vSl[:, h * 128:(h + 1) * 128].rearrange("(c p) o -> p c o", p=128), LD)

              for J in range(4):
                  spar = J % 2
                  for j in range(4 * J + 4):
                      drain_thru(ck - NS)
                      vs = 128 * max(0, j - 4 * J)
                      nkw = SPAN - vs
                      bank = sck % 4
                      sck += 1
                      pbuf = pbk % NPB
                      pbk += 1
                      slot = psk % NS
                      psk += 1
                      off = bank * SPAN
                      for ph, (ka, qa) in enumerate(((khh, qhh), (khh, qhl), (khl, qhh))):
                          v = ctx.ins("tensor", lambda Et, par=par, j=j, J=J, vs=vs, nkw=nkw, off=off, ka=ka, qa=qa, first=(ph == 0), last=(ph == 2): Et.matmul(
                              S_ps[:, off:off + nkw],
                              ka[:, par, j * 128:(j + 1) * 128],
                              qa[:, par, J * SPAN + vs:(J + 1) * SPAN],
                              start=first, stop=last),
                              sem=(PE if ph == 2 else None),
                              waits=((LD, max(q_ld[par], k_ld[par])), (SC, s_exp_rel[bank])))
                      mm_v = v
                      if j >= 4 * J:
                          mk = ctx.ins("vector", lambda Ev, off=off: Ev.tensor_tensor(
                              S_ps[:, off:off + 128], S_ps[:, off:off + 128],
                              cmaskT[:, :], ALU.add),
                              sem=DV, waits=((PE, mm_v),))
                          ew = ((DV, mk),)
                      else:
                          ew = ((PE, mm_v),)
                      ev = ctx.ins("scalar", lambda Es, pbuf=pbuf, off=off, nkw=nkw: Es.activation(
                          Pb[:, pbuf, 0:nkw], S_ps[:, off:off + nkw], AF.Exp, scale=ALPHA),
                          sem=SC,
                          waits=ew + ((DV, pb_rel_dv[pbuf]), (PE, pb_rel_pe[pbuf])))
                      s_exp_rel[bank] = ev
                      ctx.ins("vector", lambda Ev, pbuf=pbuf, slot=slot, nkw=nkw: Ev.tensor_copy(
                          PhiT[:, slot, 0:nkw], Pb[:, pbuf, 0:nkw]),
                          waits=((SC, ev), (PE, ps_rel[slot])))
                      if prev_hi[0] is not None:
                          pck, ppb, psl, pnk = prev_hi[0]
                          lo_val[pck] = ctx.ins("vector", lambda Ev, ppb=ppb, psl=psl, pnk=pnk: Ev.tensor_tensor(
                              PloT[:, psl, 0:pnk], Pb[:, ppb, 0:pnk],
                              PhiT[:, psl, 0:pnk], ALU.subtract), sem=DV)
                          pb_rel_dv[ppb] = lo_val[pck]
                      prev_hi[0] = (ck, pbuf, slot, nkw)

                      def lmm(ev=ev, spar=spar, pbuf=pbuf, vs=vs, nkw=nkw, first=(j == 0), last=(j == 4 * J + 3)):
                          v = ctx.ins("tensor", lambda Et: Et.matmul(
                              lrow_ps[spar][0:1, vs:SPAN], onescol[:, :], Pb[:, pbuf, 0:nkw],
                              start=first, stop=last),
                              sem=PE, waits=((SC, max(ev, lr_war[spar])),))
                          pb_rel_pe[pbuf] = v
                          llast[spar] = v

                      def pvu(ck=ck, par=par, j=j, J=J, slot=slot, vs=vs, nkw=nkw):
                          for ph, (vt, pt) in enumerate(((vhh, PhiT), (vhh, PloT), (vhl, PhiT))):
                              v = ctx.ins("tensor", lambda Et, vt=vt, pt=pt, first=(j == 0 and ph == 0), last=(j == 4 * J + 3 and ph == 2): Et.matmul(
                                  PV_ps[:, vs:SPAN], vt[:, par, j, :], pt[:, slot, 0:nkw],
                                  start=first, stop=last),
                                  sem=(PE if ph == 2 else None),
                                  waits=((DV, lo_val[ck]), (DV, pv_free), (LD, v_ld[par])))
                          ps_rel[slot] = v
                          kv_free[par] = v
                          pv_last[0] = v

                      bqueue.append((ck, lmm))
                      bqueue.append((ck, pvu))
                      ck += 1
                  tag = ck - 1

                  def chainA(spar=spar, par=par):
                      ln_v = ctx.ins("scalar", lambda Es: Es.activation(
                          nrowL[0:1, spar, :], lrow_ps[spar][0:1, :], AF.Ln),
                          sem=SC, waits=((PE, max(llast[spar], nrow_war[spar])),))
                      lr_war[spar] = ln_v
                      exq[spar] = ctx.ins("scalar", lambda Es: Es.activation(
                          nrow[0:1, spar, :], nrowL[0:1, spar, :], AF.Exp, scale=-1.0), sem=SC)

                  def chainB(spar=spar):
                      nonlocal chain_rel
                      bc_v = ctx.ins("tensor", lambda Et: Et.matmul(
                          chain_ps[:, 0:SPAN], onec[:, :], nrow[0:1, spar, :],
                          start=True, stop=True),
                          sem=PE, waits=((SC, max(exq[spar], chain_rel)),))
                      nrow_war[spar] = bc_v
                      nbc_rel[spar] = ctx.ins("scalar", lambda Es: Es.copy(
                          ninvbc[:, spar, :], chain_ps[:, 0:SPAN]),
                          sem=SC, waits=((PE, bc_v), (DV, nbc_war[spar])))
                      chain_rel = nbc_rel[spar]

                  def evac(h=h, J=J, spar=spar):
                      nonlocal pv_free, at_st
                      e1 = ctx.ins("vector", lambda Ev: Ev.tensor_tensor(
                          tmpn[:, :], PV_ps[:, :], ninvbc[:, spar, :], ALU.mult),
                          sem=DV, waits=((PE, pv_last[0]), (SC, nbc_rel[spar]), (ST, at_st)))
                      e2 = ctx.ins("scalar", lambda Es: Es.copy(
                          athb[:, :], tmpn[:, :]), sem=SC, waits=((DV, e1),))
                      ctx.ins("vector", lambda Ev: Ev.tensor_copy(dumm[:, :], dumm[:, :]))
                      e3 = ctx.ins("vector", lambda Ev: Ev.tensor_tensor(
                          atlb[:, :], tmpn[:, :], athb[:, :], ALU.subtract),
                          sem=DV, waits=((SC, e2),))
                      nbc_war[spar] = e1
                      pv_free = e1
                      ctx.dma("gpsimd", attnSh[h * 128:(h + 1) * 128, J * SPAN:(J + 1) * SPAN],
                              athb[:, :], ST, waits=((DV, e3),))
                      at_st = ctx.dma("gpsimd", attnSl[h * 128:(h + 1) * 128, J * SPAN:(J + 1) * SPAN],
                                      atlb[:, :], ST)

                  bqueue.append((tag, chainA))
                  bqueue.append((tag, chainB))
                  bqueue.append((tag, evac))
              hd_free[par] = PE.n
          pck, ppb, psl, pnk = prev_hi[0]
          ctx.ins("vector", lambda Ev: Ev.tensor_copy(dumm[:, :], dumm[:, :]))
          lo_val[pck] = ctx.ins("vector", lambda Ev, ppb=ppb, psl=psl, pnk=pnk: Ev.tensor_tensor(
              PloT[:, psl, 0:pnk], Pb[:, ppb, 0:pnk],
              PhiT[:, psl, 0:pnk], ALU.subtract), sem=DV)
          drain_thru(10 ** 9)
          p2_store_total = ST.n
          ctx.barrier()
      if "attn" in debug_outputs:
          pass
      if phases <= 2:
          raise _StopBuild

      with ExitStack() as p3:
          wstg = sb("wstg", [128, 2, CH, 128], p3)
          wohi = sb("wohi", [128, 8, CH, 128], p3, BF16)
          wolo = sb("wolo", [128, 8, CH, 128], p3, BF16)
          ash = sb("ash", [128, 2, CH, SPAN], p3, BF16)
          asl = sb("asl", [128, 2, CH, SPAN], p3, BF16)
          xres = sb("xres", [128, 2, SPAN], p3)
          stage3 = sb("stage3", [128, 2, SPAN], p3)
          pp3 = psum("pp3", [128, 2, SPAN], p3)
          wsplit_done = [0, 0]
          sp_free = [0, 0]
          x_free = [0, 0]
          st3_free = [0, 0]
          bank_free = [0, 0]

          def emit_wo(half, oc8, pe_gate):
              oc = half * 8 + oc8
              par = oc8 % 2
              wv = ctx.dma("sync", wstg[:, par, :, :],
                           woT[:, oc * 128:(oc + 1) * 128].rearrange("(c p) o -> p c o", p=128),
                           LD, waits=((DV, wsplit_done[par]), (PE, pe_gate)))
              ctx.ins("vector", lambda Ev, par=par, oc8=oc8: Ev.tensor_copy(
                  wohi[:, oc8, :, :], wstg[:, par, :, :]), waits=((LD, wv),))
              wsplit_done[par] = ctx.ins("vector", lambda Ev, par=par, oc8=oc8: Ev.tensor_tensor(
                  wolo[:, oc8, :, :], wstg[:, par, :, :], wohi[:, oc8, :, :],
                  ALU.subtract), sem=DV)

          for oc8 in range(CH // 2):
              emit_wo(0, oc8, 0)
          for half in range(2):
              g_wsplit = wsplit_done[(CH // 2 - 1) % 2]
              for sp in range(NSPAN):
                  spar = sp % 2
                  av = ctx.dma("sync", ash[:, spar, :, :],
                               attnSh[:, sp * SPAN:(sp + 1) * SPAN].rearrange("(c p) t -> p c t", p=128),
                               LD, waits=((PE, sp_free[spar]), (ST, p2_store_total)))
                  av = ctx.dma("sync", asl[:, spar, :, :],
                               attnSl[:, sp * SPAN:(sp + 1) * SPAN].rearrange("(c p) t -> p c t", p=128),
                               LD)
                  for oc8 in range(CH // 2):
                      oc = half * 8 + oc8
                      xpar = oc8 % 2
                      xv = ctx.dma("sync", xres[:, xpar, :],
                                   xT[oc * 128:(oc + 1) * 128, sp * SPAN:(sp + 1) * SPAN],
                                   LD, waits=((DV, x_free[xpar]),))
                      bank = oc8 % 2
                      for c in range(CH):
                          for ph, (wt, at_) in enumerate(((wohi, ash), (wohi, asl), (wolo, ash))):
                              v = ctx.ins("tensor", lambda Et, oc8=oc8, c=c, spar=spar, bank=bank, wt=wt, at_=at_, first=(c == 0 and ph == 0), last=(c == CH - 1 and ph == 2): Et.matmul(
                                  pp3[:, bank, :], wt[:, oc8, c, :],
                                  at_[:, spar, c, :],
                                  start=first, stop=last),
                                  sem=(PE if (c == CH - 1 and ph == 2) else None),
                                  waits=((LD, av), (DV, max(g_wsplit, bank_free[bank]))))
                      mmv = v
                      if half == 0 and sp == NSPAN - 1:
                          emit_wo(1, oc8, mmv)
                      if oc8 == CH // 2 - 1:
                          sp_free[spar] = mmv
                      bank_free[bank] = ctx.ins("vector", lambda Ev, xpar=xpar, bank=bank: Ev.tensor_tensor(
                          stage3[:, xpar, :], pp3[:, bank, :],
                          xres[:, xpar, :], ALU.add),
                          sem=DV, waits=((PE, mmv), (LD, xv), (ST, st3_free[xpar])))
                      x_free[xpar] = bank_free[bank]
                      st3_free[xpar] = ctx.dma(
                          "gpsimd", x1Ts[oc * 128:(oc + 1) * 128, sp * SPAN:(sp + 1) * SPAN],
                          stage3[:, xpar, :], ST, waits=((DV, bank_free[bank]),))
          p3_store_total = ST.n
          ctx.barrier()
      attn_stack.close()
      if phases <= 3:
          raise _StopBuild

      with ExitStack() as p4o:
          sel8 = sb("sel8", [8, 512], p4o)
          selE = sb("selE", [8, 512], p4o)
          ctx.ins("gpsimd", lambda Eg: Eg.iota(sel8[:, :], [[1, 8], [0, 64]], channel_multiplier=0,
                                               allow_small_or_imprecise_dtypes=True))
          ctx.ins("gpsimd", lambda Eg: Eg.iota(selE[:, :], [[0, 8], [0, 64]], channel_multiplier=1,
                                               allow_small_or_imprecise_dtypes=True), sem=GP)
          g_io4 = GP.n
          sel_done = ctx.ins("vector", lambda Ev: Ev.tensor_tensor(
              sel8[:, :], sel8[:, :], selE[:, :], ALU.is_equal), sem=DV,
              waits=((GP, g_io4),))
          aTb = sb("aTb", [128, CH, E * R], p4o)
          bSb = sb("bSb", [128, 4, D], p4o)
          wrb = sb("wrb", [128, CH, E], p4o)
          rows2 = sb("rows2", [1, 3, T], p4o)
          r2col = sb("r2col", [128, NQT], p4o)
          nwA = sb("nwA", [1, E * R], p4o)
          nwR = sb("nwR", [1, E], p4o)

          with ExitStack() as p4a:
              x1tmp = sb("x1tmp", [128, 2, T], p4a)
              xsq4 = sb("xsq4", [128, 2, T], p4a)
              st4 = psum("st4", [128, 8, SPAN], p4a)
              sq_done = [0, 0]
              last_s2 = 0
              x1_ld = []
              for c in range(CH):
                  par = c % 2
                  x1_ld.append(ctx.dma("sync", x1tmp[:, par, :], x1Ts[c * 128:(c + 1) * 128, :], LD,
                                       waits=((PE, sq_done[par]), (ST, p3_store_total))))
                  if c == 1:
                      wr_ld = ctx.dma("sync", wrb[:, :, :], wrT.rearrange("(c p) o -> p c o", p=128), LD)
                      br_ld = ctx.dma("sync", br_row[:, :], br, LD, waits=((DV, g_consts),))
                  if c == 2:
                      aT_ld = ctx.dma("sync", aTb[:, :, :], aT.rearrange("(c p) o -> p c o", p=128), LD)
                  if c == 6:
                      bS_ld = ctx.dma("sync", bSb[:, :, :], bS.rearrange("(c p) o -> p c o", p=128), LD)
                  for sp in range(NSPAN):
                      ctx.ins("tensor", lambda Et, c=c, sp=sp, par=par: Et.matmul(
                          st4[:, sp, :], onesD[:, :], x1tmp[:, par, sp * SPAN:(sp + 1) * SPAN],
                          start=(c == 0), stop=(c == CH - 1)),
                          waits=((LD, x1_ld[c]),))
                  sqv = ctx.ins("scalar", lambda Es, c=c, par=par: Es.activation(
                      xsq4[:, par, :], x1tmp[:, par, :], AF.Square),
                      sem=SC, waits=((LD, x1_ld[c]),))
                  for sp in range(NSPAN):
                      v = ctx.ins("tensor", lambda Et, par=par, sp=sp, c=c: Et.matmul(
                          st4[:, 4 + sp, :], onesD[:, :], xsq4[:, par, sp * SPAN:(sp + 1) * SPAN],
                          start=(c == 0), stop=(c == CH - 1)),
                          sem=(PE if sp == NSPAN - 1 else None), waits=((SC, sqv),))
                      if sp == NSPAN - 1:
                          sq_done[par] = v
                          last_s2 = v
              for sp in range(NSPAN):
                  ctx.ins("vector", lambda Ev, sp=sp: Ev.tensor_copy(
                      rows2[:, 0, sp * SPAN:(sp + 1) * SPAN], st4[0:1, sp, :]),
                      waits=((PE, last_s2),))
                  ctx.ins("vector", lambda Ev, sp=sp: Ev.tensor_mul(
                      rows2[:, 2, sp * SPAN:(sp + 1) * SPAN],
                      rows2[:, 0, sp * SPAN:(sp + 1) * SPAN],
                      rows2[:, 0, sp * SPAN:(sp + 1) * SPAN]))
                  ctx.ins("vector", lambda Ev, sp=sp: Ev.tensor_tensor(
                      rows2[:, 2, sp * SPAN:(sp + 1) * SPAN], st4[0:1, 4 + sp, :],
                      rows2[:, 2, sp * SPAN:(sp + 1) * SPAN], ALU.subtract), sem=DV)
              ctx.ins("vector", lambda Ev: Ev.tensor_scalar_add(rows2[:, 2, :], rows2[:, 2, :], EPS),
                      sem=DV)
              g_m2 = DV.n
              sqv = ctx.ins("scalar", lambda Es: Es.activation(
                  rows2[:, 2, :], rows2[:, 2, :], AF.Sqrt), sem=SC, waits=((DV, g_m2),))
              ctx.ins("vector", lambda Ev: Ev.reciprocal(rows2[:, 1, :], rows2[:, 2, :]),
                      sem=DV, waits=((SC, sqv),))
              g_r2 = DV.n
              cp = g_r2
              for i in range(NQT):
                  v = ctx.ins("tensor", lambda Et, i=i: Et.matmul(
                      st4[:, 0, 0:1], rows2[0:1, 1, i * 128:(i + 1) * 128], one1[:, :],
                      start=True, stop=True), sem=PE, waits=((DV, cp),))
                  cp = ctx.ins("vector", lambda Ev, i=i: Ev.tensor_copy(
                      r2col[:, i:i + 1], st4[:, 0, 0:1]), sem=DV, waits=((PE, v),))
              v = ctx.ins("tensor", lambda Et: Et.matmul(
                  st4[:, 1, 0:E], onec[:, :], br_row[:, :], start=True, stop=True),
                  sem=PE, waits=((LD, br_ld), (DV, g_m2)))
              ctx.ins("vector", lambda Ev: Ev.tensor_copy(br_bc[:, :], st4[:, 1, 0:E]),
                      sem=DV, waits=((PE, v),))
              for c in range(CH):
                  v = ctx.ins("tensor", lambda Et, c=c: Et.matmul(
                      st4[:, 2, 0:E * R], ones1[:, :], aTb[:, c, :],
                      start=(c == 0), stop=(c == CH - 1)),
                      sem=(PE if c == CH - 1 else None), waits=((LD, aT_ld), (DV, g_m2)))
              ctx.ins("vector", lambda Ev: Ev.tensor_scalar_mul(
                  nwA[:, :], st4[0:1, 2, 0:E * R], -1.0), sem=DV, waits=((PE, v),))
              for c in range(CH):
                  v = ctx.ins("tensor", lambda Et, c=c: Et.matmul(
                      st4[:, 3, 0:E], ones1[:, :], wrb[:, c, :],
                      start=(c == 0), stop=(c == CH - 1)),
                      sem=(PE if c == CH - 1 else None), waits=((LD, wr_ld), (DV, g_m2)))
              ctx.ins("vector", lambda Ev: Ev.tensor_scalar_mul(
                  nwR[:, :], st4[0:1, 3, 0:E], -1.0), sem=DV, waits=((PE, v),))
              g_rows = DV.n
              ctx.barrier()

          with ExitStack() as p4b:
              x1c = sb("x1c", [128, 2, CH, SPAN], p4b)
              hb = sb("hb", [128, 2, 4, SPAN], p4b)
              lgt = sb("lgt", [128, 2, E], p4b)
              sc_tok = sb("sc_tok", [128, 2, E], p4b)
              scT = sb("scT", [8, 2, SPAN], p4b)
              scbc = sb("scbc", [128, 2, 4, SPAN], p4b)
              tmax = sb("tmax", [128, 1], p4b)
              tsum = sb("tsum", [128, 1], p4b)
              tw = sb("tw", [128, 1], p4b)
              twr = sb("twr", [128, 1], p4b)
              vdm = sb("vdm", [128, 1], p4b)
              vdm2 = sb("vdm2", [128, 1], p4b)
              dbgA = sb("dbgA", [128, 8], p4b)
              dbgB = sb("dbgB", [128, 8], p4b)
              lgsb = sb("lgsb", [128, 8], p4b)
              ost = sb("ost", [128, 2, SPAN], p4b)
              for tz in (tmax, tsum, tw, twr, vdm, vdm2):
                  ctx.ins("vector", lambda Ev, tz=tz: Ev.memset(tz[:, :], 0.0))
              ctx.ins("vector", lambda Ev: Ev.memset(lgt[:, :, :], 0.0))
              ctx.ins("vector", lambda Ev: Ev.memset(sc_tok[:, :, :], 0.0))
              lg_ps = psum("lg_ps", [128, 2, SPAN], p4b)
              tp_ps = psum("tp_ps", [128, SPAN], p4b)
              h_ps = psum("h_ps", [128, 2, SPAN], p4b)
              o_ps = psum("o_ps", [128, 2, SPAN], p4b)

              x1_pe_free = [0, 0]
              x1_dv_free = [0, 0]
              h_free = [0, 0]
              hb_rdy = [0, 0]
              ost_free = [0, 0]
              obank_free = [0, 0]
              hbank_free = [0, 0]
              lg_rel = [0, 0]
              scT_free = [0, 0]
              scv_rel = [0, 0]
              lgtp_rel = 0
              sctp_rel = 0
              for sp in range(NSPAN):
                  par = sp % 2
                  xv = ctx.dma("sync", x1c[:, par, :, :],
                               x1Ts[:, sp * SPAN:(sp + 1) * SPAN].rearrange("(c p) t -> p c t", p=128),
                               LD, waits=((PE, x1_pe_free[par]), (DV, x1_dv_free[par]),
                                          (ST, p3_store_total)))
                  for ti in range(4):
                      tt = sp * 4 + ti
                      tslot = ti % 2
                      for c in range(CH):
                          ctx.ins("tensor", lambda Et, c=c, par=par, ti=ti, tslot=tslot: Et.matmul(
                              lg_ps[:, tslot, 0:E], x1c[:, par, c, ti * 128:(ti + 1) * 128],
                              wrb[:, c, :], start=(c == 0), stop=False),
                              waits=((LD, max(xv, wr_ld)),
                                     (DV, max(lg_rel[tslot], g_rows))))
                      v = ctx.ins("tensor", lambda Et, tt=tt, tslot=tslot: Et.matmul(
                          lg_ps[:, tslot, 0:E], rows2[0:1, 0, tt * 128:(tt + 1) * 128],
                          nwR[:, :], start=False, stop=True), sem=PE)
                      lg_rel[tslot] = ctx.ins("vector", lambda Ev, tt=tt, tslot=tslot: Ev.scalar_tensor_tensor(
                          lgt[:, tslot, :], lg_ps[:, tslot, 0:E], r2col[:, tt:tt + 1],
                          br_bc[:, :], ALU.mult, ALU.add), sem=DV,
                          waits=((PE, v), (SC, scv_rel[tslot])))
                      if "p4dbg" in debug_outputs and sp == 0 and ti == 0:
                          ctx.ins("vector", lambda Ev, tslot=tslot: Ev.tensor_copy(
                              dbgA[:, :], lgt[:, tslot, :]))
                      ctx.ins("vector", lambda Ev: Ev.tensor_copy(vdm[:, :], vdm2[:, :]))
                      ctx.ins("vector", lambda Ev, tslot=tslot: Ev.reduce_max(
                          tmax[:, :], lgt[:, tslot, :], axis=AX.X))
                      ctx.ins("vector", lambda Ev: Ev.tensor_copy(vdm[:, :], vdm2[:, :]))
                      ctx.ins("vector", lambda Ev, tslot=tslot: Ev.tensor_scalar(
                          sc_tok[:, tslot, :], lgt[:, tslot, :], tmax[:, :], None, ALU.is_equal))
                      lsub = ctx.ins("vector", lambda Ev, tslot=tslot: Ev.tensor_scalar(
                          lgt[:, tslot, :], lgt[:, tslot, :], tmax[:, :], None, ALU.subtract),
                          sem=DV)
                      scv_rel[tslot] = ctx.ins("scalar", lambda Es, tslot=tslot: Es.activation(
                          lgt[:, tslot, :], lgt[:, tslot, :], AF.Exp,
                          accum_out=tsum[:, :]), sem=SC, waits=((DV, lsub),))
                      ctx.ins("vector", lambda Ev: Ev.reciprocal(tw[:, :], tsum[:, :]),
                              waits=((SC, scv_rel[tslot]),))
                      ctx.ins("vector", lambda Ev, tt=tt: Ev.tensor_scalar(
                          twr[:, :], tw[:, :], r2col[:, tt:tt + 1], INV_RANK,
                          ALU.mult, ALU.mult))
                      ctx.ins("vector", lambda Ev: Ev.tensor_copy(vdm[:, :], vdm2[:, :]))
                      dvv = ctx.ins("vector", lambda Ev, tslot=tslot: Ev.tensor_scalar(
                          sc_tok[:, tslot, :], sc_tok[:, tslot, :], twr[:, :], None, ALU.mult),
                          sem=DV)
                      if "p4dbg" in debug_outputs and sp == 0 and ti == 0:
                          ctx.dma("gpsimd", dbgC[:, 0:8], sc_tok[:, tslot, :], ST,
                                  waits=((DV, dvv),), slow=True)
                          ctx.dma("gpsimd", dbgC[:, 8:9], tmax[:, :], ST, slow=True)
                          ctx.dma("gpsimd", dbgC[:, 9:10], twr[:, :], ST, slow=True)
                          ctx.dma("gpsimd", dbgC[:, 10:11], tw[:, :], ST, slow=True)
                          ctx.dma("gpsimd", dbgC[:, 11:12], r2col[:, 0:1], ST, slow=True)
                          ctx.dma("gpsimd", dbgP[0:128, 1024:1032], dbgA[:, :], ST,
                                  waits=((DV, lsub),))
                          ctx.dma("gpsimd", dbgP[0:128, 1056:1064], dbgB[:, :], ST,
                                  waits=((DV, lsub),))
                          ctx.dma("gpsimd", dbgP[0:128, 1040:1048], sc_tok[:, tslot, :], ST,
                                  waits=((DV, dvv),))
                      v = ctx.ins("tensor", lambda Et, tslot=tslot, ti=ti: Et.transpose(
                          tp_ps[0:8, ti * 128:(ti + 1) * 128], sc_tok[:, tslot, :], ident[:, :]),
                          sem=PE, waits=((SC, sctp_rel), (DV, dvv)))
                      sctp_rel = ctx.ins("scalar", lambda Es, par=par, ti=ti: Es.copy(
                          scT[0:8, par, ti * 128:(ti + 1) * 128],
                          tp_ps[0:8, ti * 128:(ti + 1) * 128]),
                          sem=SC, waits=((PE, max(v, scT_free[par])),))
                  scT_rdy = SC.n
                  if "p4dbg" in debug_outputs and sp == 0:
                      ctx.dma("gpsimd", dbgP[0:8, 0:SPAN], scT[0:8, par, :], ST,
                              waits=((SC, scT_rdy),))
                  for ec in range(4):
                      bank = ec % 2
                      v = ctx.ins("tensor", lambda Et, ec=ec, par=par, bank=bank: Et.matmul(
                          h_ps[:, bank, :], sel8[:, ec * 128:(ec + 1) * 128], scT[:, par, :],
                          start=True, stop=True), sem=PE,
                          waits=((SC, max(scT_rdy, hbank_free[bank])), (DV, max(hb_rdy[bank], sel_done))))
                      scT_free[par] = v
                      hbank_free[bank] = ctx.ins("scalar", lambda Es, ec=ec, par=par, bank=bank: Es.copy(
                          scbc[:, par, ec, :], h_ps[:, bank, :]), sem=SC, waits=((PE, v),))
                  if "p4dbg" in debug_outputs and sp == 0:
                      ctx.dma("gpsimd", dbgP[128:256, 0:SPAN], scbc[:, par, 0, :], ST,
                              waits=((SC, hbank_free[0]),))
                  for ec in range(4):
                      bank = ec % 2
                      for c in range(CH):
                          ctx.ins("tensor", lambda Et, c=c, ec=ec, par=par, bank=bank: Et.matmul(
                              h_ps[:, bank, :], aTb[:, c, ec * 128:(ec + 1) * 128],
                              x1c[:, par, c, :], start=(c == 0), stop=False),
                              waits=((LD, max(xv, aT_ld)), (SC, hbank_free[bank]),
                                     (DV, hb_rdy[bank])))
                      v = ctx.ins("tensor", lambda Et, ec=ec, sp=sp, bank=bank: Et.matmul(
                          h_ps[:, bank, :], nwA[:, ec * 128:(ec + 1) * 128],
                          rows2[0:1, 0, sp * SPAN:(sp + 1) * SPAN], start=False, stop=True),
                          sem=PE)
                      hb_rdy[bank] = ctx.ins("vector", lambda Ev, ec=ec, par=par, bank=bank: Ev.tensor_tensor(
                          hb[:, par, ec, :], h_ps[:, bank, :], scbc[:, par, ec, :], ALU.mult),
                          sem=DV, waits=((PE, max(v, h_free[par])),))
                  h_rdy = DV.n
                  if "p4dbg" in debug_outputs and sp == 0:
                      ctx.dma("gpsimd", dbgP[256:384, 0:SPAN], hb[:, par, 0, :], ST,
                              waits=((DV, h_rdy),))
                  for oc in range(CH):
                      bank = oc % 2
                      opar = oc % 2
                      for ec in range(4):
                          v = ctx.ins("tensor", lambda Et, ec=ec, oc=oc, par=par, bank=bank: Et.matmul(
                              o_ps[:, bank, :], bSb[:, ec, oc * 128:(oc + 1) * 128],
                              hb[:, par, ec, :], start=(ec == 0), stop=(ec == 3)),
                              sem=(PE if ec == 3 else None),
                              waits=((DV, max(h_rdy, obank_free[bank])), (LD, bS_ld)))
                      mmv = v
                      if oc == CH - 1:
                          h_free[par] = mmv
                      obank_free[bank] = ctx.ins("vector", lambda Ev, oc=oc, par=par, bank=bank, opar=opar: Ev.tensor_tensor(
                          ost[:, opar, :], o_ps[:, bank, :], x1c[:, par, oc, :], ALU.add),
                          sem=DV, waits=((PE, mmv), (ST, ost_free[opar])))
                      ost_free[opar] = ctx.dma(
                          "gpsimd", outT[oc * 128:(oc + 1) * 128, sp * SPAN:(sp + 1) * SPAN],
                          ost[:, opar, :], ST, waits=((DV, obank_free[bank]),))
                  x1_pe_free[par] = PE.n
                  x1_dv_free[par] = DV.n

              ctx.wait("gpsimd", ST, ST.n)
    except _StopBuild:
        attn_stack.close()
        ctx.wait("gpsimd", ST, ST.n)

    with nc.Block() as block:
        for eng in ENGINES:
            def body(Eng, eng=eng):
                for fn in ctx.streams[eng]:
                    fn(Eng)
            getattr(block, eng)(body)
    es.close()
    return nc


def _prep_inputs(L0_attn_x, L0_attn_Wq, L0_attn_Wk, L0_attn_Wv, L0_attn_Wo,
                 L0_ffn_A, L0_ffn_B, L0_ffn_Wr, L0_ffn_br):
    C = np.ascontiguousarray
    f = np.float32
    wqT = C(np.asarray(L0_attn_Wq, dtype=f).T)
    wkT = C(np.asarray(L0_attn_Wk, dtype=f).T)
    wvT = C(np.asarray(L0_attn_Wv, dtype=f).T)
    woT = C(np.asarray(L0_attn_Wo, dtype=f).T)
    aT = C(np.asarray(L0_ffn_A, dtype=f).reshape(E * R, D).T)
    bSm = C(np.asarray(L0_ffn_B, dtype=f).transpose(0, 2, 1).reshape(E * R, D))
    wrT = C(np.asarray(L0_ffn_Wr, dtype=f).T)
    brm = C(np.asarray(L0_ffn_br, dtype=f).reshape(1, E))
    x = np.asarray(L0_attn_x, dtype=f)
    in_maps = []
    for b in range(8):
        in_maps.append({
            "xT": C(x[b].T),
            "wqT": wqT, "wkT": wkT, "wvT": wvT, "woT": woT,
            "aT": aT, "bS": bSm, "wrT": wrT, "br": brm,
        })
    return in_maps


def kernel(L0_attn_x, L0_attn_Wq, L0_attn_Wk, L0_attn_Wv, L0_attn_Wo,
           L0_ffn_A, L0_ffn_B, L0_ffn_Wr, L0_ffn_br):
    in_maps = _prep_inputs(L0_attn_x, L0_attn_Wq, L0_attn_Wk, L0_attn_Wv, L0_attn_Wo,
                           L0_ffn_A, L0_ffn_B, L0_ffn_Wr, L0_ffn_br)
    nc = build_nc()
    res = run_bass_kernel_spmd(nc, in_maps, core_ids=list(range(8)))
    out = np.stack([np.asarray(r["outT"], dtype=np.float32).T for r in res.results], axis=0)
    return np.ascontiguousarray(out)


# version 16
# speedup vs baseline: 1.0075x; 1.0075x over previous
import numpy as np
from contextlib import ExitStack
import concourse.bass as bass
import concourse.mybir as mybir
from concourse.bass_utils import run_bass_kernel_spmd

F32 = mybir.dt.float32
BF16 = mybir.dt.bfloat16
AF = mybir.ActivationFunctionType
ALU = mybir.AluOpType
AX = mybir.AxisListType

D = 2048
T = 2048
CH = 16
NQT = 16
SPAN = 512
NSPAN = 4
H = 16
E = 8
R = 64
ALPHA = float(1.0 / np.sqrt(128.0))
EPS = 1e-5
MASKNEG = -1.0e30
INV_D = float(1.0 / D)
INV_RANK = float(1.0 / R)

ENGINES = ("sync", "tensor", "vector", "scalar", "gpsimd")


class _Sem:
    def __init__(self, name, handle):
        self.name = name
        self.h = handle
        self.n = 0


class _Ctx:
    def __init__(self, nc, sems):
        self.nc = nc
        self.streams = {e: [] for e in ENGINES}
        self.sems = sems
        self._lastw = {}
        self.strict = False

    def wait(self, eng, sem, val):
        if val is None or val <= 0:
            return
        key = (eng, sem.name)
        if self._lastw.get(key, 0) >= val:
            return
        self._lastw[key] = val
        self.streams[eng].append(lambda Eng, h=sem.h, v=val: Eng.wait_ge(h, v))

    def ins(self, eng, fn, sem=None, k=1, waits=()):
        if eng == "vector":
            self._vn = getattr(self, "_vn", 0) + 1
            import sys as _sys
            f = _sys._getframe(1)
            if not hasattr(self, "vlog"):
                self.vlog = {}
            self.vlog[self._vn] = f.f_lineno
        do_strict = (self.strict is True or
                     (isinstance(self.strict, (set, frozenset)) and eng in self.strict) or
                     (isinstance(self.strict, tuple) and eng == "vector"
                      and self.strict[0] <= self._vn < self.strict[1]))
        if do_strict:
            for s in self.sems.values():
                self.wait(eng, s, s.n)
        for (s, v) in waits:
            self.wait(eng, s, v)
        if sem is None:
            self.streams[eng].append(fn)
            return None
        sem.n += k
        v = sem.n
        self.streams[eng].append(lambda Eng, f=fn, h=sem.h, kk=k: f(Eng).then_inc(h, kk))
        return v

    def dma(self, eng, out_ap, in_ap, sem, waits=(), slow=False):
        if slow:
            return self.ins(eng, lambda Eng, o=out_ap, i=in_ap: Eng.dma_start(
                out=o, in_=i, allow_slow_non_contiguous=True), sem=sem, k=16, waits=waits)
        return self.ins(eng, lambda Eng, o=out_ap, i=in_ap: Eng.dma_start(out=o, in_=i),
                        sem=sem, k=16, waits=waits)

    def barrier(self):
        snap = {name: s.n for name, s in self.sems.items()}
        for eng in ENGINES:
            for name, v in snap.items():
                self.wait(eng, self.sems[name], v)


class _StopBuild(Exception):
    pass


def build_nc(debug_outputs=(), phases=4, strict=False):
    nc = bass.Bass("TRN2", target_bir_lowering=False)

    xT = nc.dram_tensor("xT", [D, T], F32, kind="ExternalInput").ap()
    wqT = nc.dram_tensor("wqT", [D, D], F32, kind="ExternalInput").ap()
    wkT = nc.dram_tensor("wkT", [D, D], F32, kind="ExternalInput").ap()
    wvT = nc.dram_tensor("wvT", [D, D], F32, kind="ExternalInput").ap()
    woT = nc.dram_tensor("woT", [D, D], F32, kind="ExternalInput").ap()
    aT = nc.dram_tensor("aT", [D, E * R], F32, kind="ExternalInput").ap()
    bS = nc.dram_tensor("bS", [E * R, D], F32, kind="ExternalInput").ap()
    wrT = nc.dram_tensor("wrT", [D, E], F32, kind="ExternalInput").ap()
    br = nc.dram_tensor("br", [1, E], F32, kind="ExternalInput").ap()
    outT = nc.dram_tensor("outT", [D, T], F32, kind="ExternalOutput").ap()

    def scratch(name, dtype=F32):
        kind = "ExternalOutput" if name in debug_outputs else "Internal"
        return nc.dram_tensor(name + "_s", [D, T], dtype, kind=kind).ap()

    qTsh = scratch("qh", BF16)
    qTsl = scratch("ql", BF16)
    kTsh = scratch("kh", BF16)
    kTsl = scratch("kl", BF16)
    vSh = scratch("vh", BF16)
    vSl = scratch("vl", BF16)
    attnSh = scratch("ah", BF16)
    attnSl = scratch("al", BF16)
    x1Ts = scratch("x1")
    nrow_s = nc.dram_tensor("nrow_s", [1, 512], F32, kind="Internal").ap()
    dbgP = nc.dram_tensor("dbgP_s", [3 * 128, T], F32,
                          kind="ExternalOutput" if ("p2dbg" in debug_outputs or "p4dbg" in debug_outputs) else "Internal").ap()
    dbgC = nc.dram_tensor("dbgC_s", [128, 16], F32,
                          kind="ExternalOutput" if ("p2dbg" in debug_outputs or "p4dbg" in debug_outputs) else "Internal").ap()

    es = ExitStack()
    sems = {}
    for name in ("LD", "ST", "PE", "DV", "SC", "GP"):
        sems[name] = _Sem(name, es.enter_context(nc.semaphore(name)))
    LD, ST, PE, DV, SC, GP = (sems[k] for k in ("LD", "ST", "PE", "DV", "SC", "GP"))
    ctx = _Ctx(nc, sems)
    ctx.strict = strict

    def sb(name, shape, stack=es, dtype=F32):
        return stack.enter_context(nc.sbuf_tensor(name, shape, dtype))

    def psum(name, shape, stack=es):
        return stack.enter_context(nc.psum_tensor(name, shape, F32))

    ident = sb("ident", [128, 128])
    cmaskT = sb("cmaskT", [128, 128])
    onesD = sb("onesD", [128, 128])
    ones1 = sb("ones1", [128, 128])
    onec = sb("onec", [1, 128])
    onescol = sb("onescol", [128, 1])
    one1 = sb("one1", [1, 1])
    br_bc = sb("br_bc", [128, E])
    br_row = sb("br_row", [1, E])
    ioA = sb("ioA", [128, 128])
    ioB = sb("ioB", [128, 128])

    ctx.ins("gpsimd", lambda Eg: Eg.iota(ioA[:, :], [[1, 128]], channel_multiplier=0,
                                         allow_small_or_imprecise_dtypes=True))
    ctx.ins("gpsimd", lambda Eg: Eg.iota(ioB[:, :], [[0, 128]], channel_multiplier=1,
                                         allow_small_or_imprecise_dtypes=True))
    ctx.ins("gpsimd", lambda Eg: Eg.iota(selE0 if False else ioA[:, :], [[1, 128]], channel_multiplier=0,
                                         allow_small_or_imprecise_dtypes=True), sem=GP)
    g_iota = GP.n
    ctx.ins("vector", lambda Ev: Ev.tensor_tensor(ident[:, :], ioA[:, :], ioB[:, :], ALU.is_equal),
            waits=((GP, g_iota),))
    ctx.ins("vector", lambda Ev: Ev.tensor_tensor(cmaskT[:, :], ioB[:, :], ioA[:, :], ALU.is_gt))
    ctx.ins("vector", lambda Ev: Ev.tensor_scalar_mul(cmaskT[:, :], cmaskT[:, :], MASKNEG))
    ctx.ins("vector", lambda Ev: Ev.memset(onesD[:, :], INV_D))
    ctx.ins("vector", lambda Ev: Ev.memset(ones1[:, :], 1.0))
    ctx.ins("vector", lambda Ev: Ev.memset(onec[:, :], 1.0))
    ctx.ins("vector", lambda Ev: Ev.memset(onescol[:, :], 1.0))
    ctx.ins("vector", lambda Ev: Ev.memset(one1[:, :], 1.0), sem=DV)
    g_consts = DV.n

    attn_stack = ExitStack()
    try:
      with ExitStack() as p1o:
          xhi = sb("xhi", [128, CH, T], p1o, BF16)
          xlo = sb("xlo", [128, CH, T], p1o, BF16)
          r1bc = sb("r1bc", [128, T], p1o)
          r1col = sb("r1col", [128, NQT], p1o)

          with ExitStack() as p1a:
              xst = sb("xst", [128, 2, T], p1a)
              xsq = sb("xsq", [128, 2, T], p1a)
              m1bc = sb("m1bc", [128, T], p1a)
              st_ps = psum("st_ps", [128, 8, SPAN], p1a)

              sq_done = [0, 0]
              last_s2 = 0
              last_s1 = [0, 0]
              for c in range(CH):
                  par = c % 2
                  xv = ctx.dma("sync", xst[:, par, :], xT[c * 128:(c + 1) * 128, :], LD,
                               waits=((PE, max(sq_done[par], last_s1[par])),))
                  for sp in range(NSPAN):
                      v = ctx.ins("tensor", lambda Et, c=c, sp=sp, par=par: Et.matmul(
                          st_ps[:, sp, :], onesD[:, :], xst[:, par, sp * SPAN:(sp + 1) * SPAN],
                          start=(c == 0), stop=(c == CH - 1)),
                          sem=(PE if sp == NSPAN - 1 else None),
                          waits=((LD, xv), (DV, g_consts)))
                  last_s1[par] = v
                  sqv = ctx.ins("scalar", lambda Es, c=c, par=par: Es.activation(
                      xsq[:, par, :], xst[:, par, :], AF.Square),
                      sem=SC, waits=((LD, xv), (PE, sq_done[par])))
                  for sp in range(NSPAN):
                      v = ctx.ins("tensor", lambda Et, par=par, sp=sp, c=c: Et.matmul(
                          st_ps[:, 4 + sp, :], onesD[:, :], xsq[:, par, sp * SPAN:(sp + 1) * SPAN],
                          start=(c == 0), stop=(c == CH - 1)),
                          sem=(PE if sp == NSPAN - 1 else None), waits=((SC, sqv),))
                      if sp == NSPAN - 1:
                          sq_done[par] = v
                          last_s2 = v

              for sp in range(NSPAN):
                  ctx.ins("vector", lambda Ev, sp=sp: Ev.tensor_copy(
                      m1bc[:, sp * SPAN:(sp + 1) * SPAN], st_ps[:, sp, :]),
                      waits=((PE, last_s2),))
                  ctx.ins("vector", lambda Ev, sp=sp: Ev.tensor_mul(
                      xsq[:, 0, sp * SPAN:(sp + 1) * SPAN],
                      m1bc[:, sp * SPAN:(sp + 1) * SPAN], m1bc[:, sp * SPAN:(sp + 1) * SPAN]))
                  ctx.ins("vector", lambda Ev, sp=sp: Ev.tensor_tensor(
                      r1bc[:, sp * SPAN:(sp + 1) * SPAN], st_ps[:, 4 + sp, :],
                      xsq[:, 0, sp * SPAN:(sp + 1) * SPAN], ALU.subtract), sem=DV)
              ctx.ins("vector", lambda Ev: Ev.tensor_scalar_add(r1bc[:, :], r1bc[:, :], EPS),
                      sem=DV)
              g_var = DV.n
              sqv = ctx.ins("scalar", lambda Es: Es.activation(
                  r1bc[:, :], r1bc[:, :], AF.Sqrt), sem=SC, waits=((DV, g_var),))
              ctx.ins("vector", lambda Ev: Ev.reciprocal(r1bc[:, :], r1bc[:, :]),
                      sem=DV, waits=((SC, sqv),))
              g_r1 = DV.n
              cp = g_r1
              for i in range(NQT):
                  v = ctx.ins("tensor", lambda Et, i=i: Et.matmul(
                      st_ps[:, 0, 0:1], r1bc[0:1, i * 128:(i + 1) * 128], one1[:, :],
                      start=True, stop=True), sem=PE, waits=((DV, cp),))
                  cp = ctx.ins("vector", lambda Ev, i=i: Ev.tensor_copy(
                      r1col[:, i:i + 1], st_ps[:, 0, 0:1]), sem=DV, waits=((PE, v),))
              sp_done = [g_r1, g_r1]
              for c in range(CH):
                  par = c % 2
                  xv = ctx.dma("sync", xst[:, par, :], xT[c * 128:(c + 1) * 128, :], LD,
                               waits=((DV, sp_done[par]),))
                  ctx.ins("vector", lambda Ev, c=c, par=par: Ev.tensor_tensor(
                      xst[:, par, :], xst[:, par, :], m1bc[:, :], ALU.subtract),
                      waits=((LD, xv),))
                  ctx.ins("vector", lambda Ev, c=c, par=par: Ev.tensor_copy(
                      xhi[:, c, :], xst[:, par, :]))
                  sp_done[par] = ctx.ins("vector", lambda Ev, c=c, par=par: Ev.tensor_tensor(
                      xlo[:, c, :], xst[:, par, :], xhi[:, c, :], ALU.subtract), sem=DV)
              ctx.barrier()

          with ExitStack() as p1b:
              wbuf = sb("wbuf", [128, 2, CH, 128], p1b)
              whi = sb("whi", [128, 2, CH, 128], p1b, BF16)
              wlo = sb("wlo", [128, 2, CH, 128], p1b, BF16)
              stage = sb("stage", [128, 2, SPAN], p1b)
              stageh = sb("stageh", [128, 2, T], p1b, BF16)
              stagel = sb("stagel", [128, 2, T], p1b, BF16)
              pp = psum("pp", [128, 2, SPAN], p1b)
              wb_free = [0, 0]
              wsp_done = [0, 0]
              st_free = [0, 0]
              bank_free = [0, 0]
              for wsrc, wdsth, wdstl in ((wqT, qTsh, qTsl), (wkT, kTsh, kTsl)):
                  for oc in range(CH):
                      par = oc % 2
                      wv = ctx.dma("sync", wbuf[:, par, :, :],
                                   wsrc[:, oc * 128:(oc + 1) * 128].rearrange("(c p) o -> p c o", p=128),
                                   LD, waits=((DV, wsp_done[par]),))
                      ctx.ins("vector", lambda Ev, par=par: Ev.tensor_copy(
                          whi[:, par, :, :], wbuf[:, par, :, :]),
                          waits=((LD, wv), (PE, wb_free[par])))
                      wsp_done[par] = ctx.ins("vector", lambda Ev, par=par: Ev.tensor_tensor(
                          wlo[:, par, :, :], wbuf[:, par, :, :], whi[:, par, :, :],
                          ALU.subtract), sem=DV)
                      evs = 0
                      for sp in range(NSPAN):
                          bank = sp % 2
                          for c in range(CH):
                              for ph, (wt, xt_) in enumerate(((whi, xhi), (whi, xlo), (wlo, xhi))):
                                  v = ctx.ins("tensor", lambda Et, par=par, c=c, sp=sp, bank=bank, wt=wt, xt_=xt_, first=(c == 0 and ph == 0), last=(c == CH - 1 and ph == 2): Et.matmul(
                                      pp[:, bank, :], wt[:, par, c, :],
                                      xt_[:, c, sp * SPAN:(sp + 1) * SPAN],
                                      start=first, stop=last),
                                      sem=(PE if (c == CH - 1 and ph == 2) else None),
                                      waits=((DV, max(wsp_done[par], bank_free[bank])),))
                          mmv = v
                          if sp == NSPAN - 1:
                              wb_free[par] = mmv
                          ctx.ins("vector", lambda Ev, par=par, sp=sp, bank=bank: Ev.tensor_tensor(
                              stage[:, par, :], pp[:, bank, :],
                              r1bc[:, sp * SPAN:(sp + 1) * SPAN], ALU.mult),
                              waits=((PE, mmv), (ST, st_free[par])))
                          ctx.ins("vector", lambda Ev, par=par, sp=sp: Ev.tensor_copy(
                              stageh[:, par, sp * SPAN:(sp + 1) * SPAN],
                              stage[:, par, :]))
                          bank_free[bank] = ctx.ins("vector", lambda Ev, par=par, sp=sp: Ev.tensor_tensor(
                              stagel[:, par, sp * SPAN:(sp + 1) * SPAN],
                              stage[:, par, :],
                              stageh[:, par, sp * SPAN:(sp + 1) * SPAN], ALU.subtract),
                              sem=DV)
                          evs = bank_free[bank]
                      ctx.dma("gpsimd", wdsth[oc * 128:(oc + 1) * 128, :],
                              stageh[:, par, :], ST, waits=((DV, evs),))
                      st_free[par] = ctx.dma("gpsimd", wdstl[oc * 128:(oc + 1) * 128, :],
                                             stagel[:, par, :], ST)
              ctx.barrier()

          with ExitStack() as p1c:
              VOS = 512
              vw = r1bc[:, :].rearrange("p (c o) -> p c o", o=128)
              vdum = sb("vdum", [128, 1], p1c)
              vwhi = sb("vwhi", [128, 2, CH, VOS], p1c, BF16)
              vwlo = sb("vwlo", [128, 2, CH, VOS], p1c, BF16)
              vstageh = sb("vstageh", [128, 2, SPAN], p1c, BF16)
              vstagel = sb("vstagel", [128, 2, SPAN], p1c, BF16)
              ppv = psum("ppv", [128, 2, SPAN], p1c)
              vsp_done = DV.n
              vsp_blk = [0, 0]
              blk_mm = [0, 0]
              vst_free = [0, 0]
              vbank_free = [0, 0]

              def vw_load(os_, q):
                  return ctx.dma("sync", vw,
                                 wvT[:, os_ * VOS + q * 128:os_ * VOS + (q + 1) * 128].rearrange("(c p) o -> p c o", p=128),
                                 LD, waits=((DV, vsp_done),))

              def vw_split(os_, q, wv):
                  nonlocal vsp_done
                  wpar = os_ % 2
                  ctx.ins("vector", lambda Ev, q=q, wpar=wpar: Ev.tensor_copy(
                      vwhi[:, wpar, :, q * 128:(q + 1) * 128], vw),
                      waits=((LD, wv), (PE, blk_mm[wpar])))
                  vsp_done = ctx.ins("vector", lambda Ev, q=q, wpar=wpar: Ev.tensor_tensor(
                      vwlo[:, wpar, :, q * 128:(q + 1) * 128], vw,
                      vwhi[:, wpar, :, q * 128:(q + 1) * 128], ALU.subtract), sem=DV)
                  if q == 3:
                      vsp_blk[wpar] = vsp_done

              for q in range(4):
                  _wv = vw_load(0, q)
                  vw_split(0, q, _wv)
              pend_wv = [0]
              for os_ in range(4):
                  wpar = os_ % 2
                  for tt in range(NQT):
                      if os_ + 1 < 4:
                          if tt == 1:
                              pend_wv[0] = vw_load(os_ + 1, 0)
                          elif tt in (3, 5, 7):
                              vw_split(os_ + 1, (tt - 3) // 2, pend_wv[0])
                              pend_wv[0] = vw_load(os_ + 1, (tt - 1) // 2)
                          elif tt == 9:
                              vw_split(os_ + 1, 3, pend_wv[0])
                      bank = tt % 2
                      for c in range(CH):
                          for ph, (xt_, wt) in enumerate(((xhi, vwhi), (xhi, vwlo), (xlo, vwhi))):
                              v = ctx.ins("tensor", lambda Et, c=c, tt=tt, bank=bank, wpar=wpar, xt_=xt_, wt=wt, first=(c == 0 and ph == 0), last=(c == CH - 1 and ph == 2): Et.matmul(
                                  ppv[:, bank, 0:VOS], xt_[:, c, tt * 128:(tt + 1) * 128], wt[:, wpar, c, :],
                                  start=first, stop=last),
                                  sem=(PE if (c == CH - 1 and ph == 2) else None),
                                  waits=((DV, max(vsp_blk[wpar], vbank_free[bank])),))
                      mmv = v
                      if tt == NQT - 1:
                          blk_mm[wpar] = mmv
                      spar = tt % 2
                      ctx.ins("vector", lambda Ev, spar=spar, bank=bank, tt=tt: Ev.tensor_scalar(
                          vstageh[:, spar, 0:VOS], ppv[:, bank, 0:VOS], r1col[:, tt:tt + 1], None, ALU.mult),
                          waits=((PE, mmv), (ST, vst_free[spar])))
                      ctx.ins("vector", lambda Ev: Ev.tensor_copy(vdum[:, :], vdum[:, :]))
                      vbank_free[bank] = ctx.ins("vector", lambda Ev, spar=spar, bank=bank, tt=tt: Ev.scalar_tensor_tensor(
                          vstagel[:, spar, 0:VOS], ppv[:, bank, 0:VOS], r1col[:, tt:tt + 1],
                          vstageh[:, spar, 0:VOS], ALU.mult, ALU.subtract), sem=DV)
                      ctx.dma("gpsimd", vSh[tt * 128:(tt + 1) * 128, os_ * VOS:(os_ + 1) * VOS],
                              vstageh[:, spar, 0:VOS], ST, waits=((DV, vbank_free[bank]),))
                      vst_free[spar] = ctx.dma(
                          "gpsimd", vSl[tt * 128:(tt + 1) * 128, os_ * VOS:(os_ + 1) * VOS],
                          vstagel[:, spar, 0:VOS], ST)
              p1_store_total = ST.n
              ctx.barrier()
      if phases <= 1:
          raise _StopBuild

      with ExitStack() as p2:
          qhh = sb("qhh", [128, 2, T], p2, BF16)
          qhl = sb("qhl", [128, 2, T], p2, BF16)
          khh = sb("khh", [128, 2, T], p2, BF16)
          khl = sb("khl", [128, 2, T], p2, BF16)
          vhh = sb("vhh", [128, 2, NQT, 128], p2, BF16)
          vhl = sb("vhl", [128, 2, NQT, 128], p2, BF16)
          NPB = 4
          NS = 4
          Pb = sb("Pb", [128, NPB, SPAN], p2)
          PhiT = sb("PhiT", [128, NS, SPAN], p2, BF16)
          PloT = sb("PloT", [128, NS, SPAN], p2, BF16)
          athb = sb("athb", [128, SPAN], p2, BF16)
          atlb = sb("atlb", [128, SPAN], p2, BF16)
          tmpn = sb("tmpn", [128, SPAN], p2)
          nrowL = sb("nrowL", [1, 2, SPAN], p2)
          nrow = sb("nrow", [1, 2, SPAN], p2)
          ninvbc = sb("ninvbc", [128, 2, SPAN], p2)
          dumm = sb("dumm", [128, 1], p2)
          S_ps = psum("S_ps", [128, 4 * SPAN], p2)
          PV_ps = psum("PV_ps", [128, SPAN], p2)
          chain_ps = psum("chain_ps", [128, SPAN], p2)
          lrow_ps = [psum("lrA", [1, SPAN], p2), psum("lrB", [1, SPAN], p2)]

          ctx.ins("vector", lambda Ev: Ev.memset(dumm[:, :], 0.0))
          ctx.ins("vector", lambda Ev: Ev.memset(ninvbc[:, :, :], 0.0), sem=DV)

          s_exp_rel = [0, 0, 0, 0]
          pb_rel_dv = [0] * NPB
          pb_rel_pe = [0] * NPB
          ps_rel = [0] * NS
          pv_free = 0
          at_st = 0
          nbc_rel = [0, 0]
          nbc_war = [0, 0]
          lr_war = [0, 0]
          nrow_war = [0, 0]
          chain_rel = 0
          hd_free = [0, 0]
          kv_free = [0, 0]
          q_ld = [0, 0]
          k_ld = [0, 0]
          v_ld = [0, 0]
          sck = 0
          pbk = 0
          psk = 0
          ck = 0
          lo_val = {}
          llast = [0, 0]
          pv_last = [0]
          exq = [None, None]
          bqueue = []
          prev_hi = [None]

          def drain_thru(tag):
              while bqueue and bqueue[0][0] <= tag:
                  bqueue.pop(0)[1]()

          for h in range(H):
              par = h % 2
              q_ld[par] = ctx.dma("sync", qhh[:, par, :], qTsh[h * 128:(h + 1) * 128, :], LD,
                                  waits=((PE, hd_free[par]), (ST, p1_store_total)))
              q_ld[par] = ctx.dma("sync", qhl[:, par, :], qTsl[h * 128:(h + 1) * 128, :], LD)
              ctx.wait("sync", PE, kv_free[par])
              k_ld[par] = ctx.dma("sync", khh[:, par, :], kTsh[h * 128:(h + 1) * 128, :], LD)
              k_ld[par] = ctx.dma("sync", khl[:, par, :], kTsl[h * 128:(h + 1) * 128, :], LD)
              v_ld[par] = ctx.dma("sync", vhh[:, par, :, :],
                             vSh[:, h * 128:(h + 1) * 128].rearrange("(c p) o -> p c o", p=128), LD)
              v_ld[par] = ctx.dma("sync", vhl[:, par, :, :],
                             vSl[:, h * 128:(h + 1) * 128].rearrange("(c p) o -> p c o", p=128), LD)

              for J in range(4):
                  spar = J % 2
                  for j in range(4 * J + 4):
                      drain_thru(ck - NS)
                      vs = 128 * max(0, j - 4 * J)
                      nkw = SPAN - vs
                      bank = sck % 4
                      sck += 1
                      pbuf = pbk % NPB
                      pbk += 1
                      slot = psk % NS
                      psk += 1
                      off = bank * SPAN
                      for ph, (ka, qa) in enumerate(((khh, qhh), (khh, qhl), (khl, qhh))):
                          v = ctx.ins("tensor", lambda Et, par=par, j=j, J=J, vs=vs, nkw=nkw, off=off, ka=ka, qa=qa, first=(ph == 0), last=(ph == 2): Et.matmul(
                              S_ps[:, off:off + nkw],
                              ka[:, par, j * 128:(j + 1) * 128],
                              qa[:, par, J * SPAN + vs:(J + 1) * SPAN],
                              start=first, stop=last),
                              sem=(PE if ph == 2 else None),
                              waits=((LD, max(q_ld[par], k_ld[par])), (SC, s_exp_rel[bank])))
                      mm_v = v
                      if j >= 4 * J:
                          mk = ctx.ins("vector", lambda Ev, off=off: Ev.tensor_tensor(
                              S_ps[:, off:off + 128], S_ps[:, off:off + 128],
                              cmaskT[:, :], ALU.add),
                              sem=DV, waits=((PE, mm_v),))
                          ew = ((DV, mk),)
                      else:
                          ew = ((PE, mm_v),)
                      ev = ctx.ins("scalar", lambda Es, pbuf=pbuf, off=off, nkw=nkw: Es.activation(
                          Pb[:, pbuf, 0:nkw], S_ps[:, off:off + nkw], AF.Exp, scale=ALPHA),
                          sem=SC,
                          waits=ew + ((DV, pb_rel_dv[pbuf]), (PE, pb_rel_pe[pbuf])))
                      s_exp_rel[bank] = ev
                      ctx.ins("vector", lambda Ev, pbuf=pbuf, slot=slot, nkw=nkw: Ev.tensor_copy(
                          PhiT[:, slot, 0:nkw], Pb[:, pbuf, 0:nkw]),
                          waits=((SC, ev), (PE, ps_rel[slot])))
                      if prev_hi[0] is not None:
                          pck, ppb, psl, pnk = prev_hi[0]
                          lo_val[pck] = ctx.ins("vector", lambda Ev, ppb=ppb, psl=psl, pnk=pnk: Ev.tensor_tensor(
                              PloT[:, psl, 0:pnk], Pb[:, ppb, 0:pnk],
                              PhiT[:, psl, 0:pnk], ALU.subtract), sem=DV)
                          pb_rel_dv[ppb] = lo_val[pck]
                      prev_hi[0] = (ck, pbuf, slot, nkw)

                      def lmm(ev=ev, spar=spar, pbuf=pbuf, vs=vs, nkw=nkw, first=(j == 0), last=(j == 4 * J + 3)):
                          v = ctx.ins("tensor", lambda Et: Et.matmul(
                              lrow_ps[spar][0:1, vs:SPAN], onescol[:, :], Pb[:, pbuf, 0:nkw],
                              start=first, stop=last),
                              sem=PE, waits=((SC, max(ev, lr_war[spar])),))
                          pb_rel_pe[pbuf] = v
                          llast[spar] = v

                      def pvu(ck=ck, par=par, j=j, J=J, slot=slot, vs=vs, nkw=nkw):
                          for ph, (vt, pt) in enumerate(((vhh, PhiT), (vhh, PloT), (vhl, PhiT))):
                              v = ctx.ins("tensor", lambda Et, vt=vt, pt=pt, first=(j == 0 and ph == 0), last=(j == 4 * J + 3 and ph == 2): Et.matmul(
                                  PV_ps[:, vs:SPAN], vt[:, par, j, :], pt[:, slot, 0:nkw],
                                  start=first, stop=last),
                                  sem=(PE if ph == 2 else None),
                                  waits=((DV, lo_val[ck]), (DV, pv_free), (LD, v_ld[par])))
                          ps_rel[slot] = v
                          kv_free[par] = v
                          pv_last[0] = v

                      bqueue.append((ck, lmm))
                      bqueue.append((ck, pvu))
                      ck += 1
                  tag = ck - 1

                  def chainA(spar=spar, par=par):
                      ln_v = ctx.ins("scalar", lambda Es: Es.activation(
                          nrowL[0:1, spar, :], lrow_ps[spar][0:1, :], AF.Ln),
                          sem=SC, waits=((PE, max(llast[spar], nrow_war[spar])),))
                      lr_war[spar] = ln_v
                      exq[spar] = ctx.ins("scalar", lambda Es: Es.activation(
                          nrow[0:1, spar, :], nrowL[0:1, spar, :], AF.Exp, scale=-1.0), sem=SC)

                  def chainB(spar=spar):
                      nonlocal chain_rel
                      bc_v = ctx.ins("tensor", lambda Et: Et.matmul(
                          chain_ps[:, 0:SPAN], onec[:, :], nrow[0:1, spar, :],
                          start=True, stop=True),
                          sem=PE, waits=((SC, max(exq[spar], chain_rel)),))
                      nrow_war[spar] = bc_v
                      nbc_rel[spar] = ctx.ins("scalar", lambda Es: Es.copy(
                          ninvbc[:, spar, :], chain_ps[:, 0:SPAN]),
                          sem=SC, waits=((PE, bc_v), (DV, nbc_war[spar])))
                      chain_rel = nbc_rel[spar]

                  def evac(h=h, J=J, spar=spar):
                      nonlocal pv_free, at_st
                      e1 = ctx.ins("vector", lambda Ev: Ev.tensor_tensor(
                          tmpn[:, :], PV_ps[:, :], ninvbc[:, spar, :], ALU.mult),
                          sem=DV, waits=((PE, pv_last[0]), (SC, nbc_rel[spar]), (ST, at_st)))
                      e2 = ctx.ins("scalar", lambda Es: Es.copy(
                          athb[:, :], tmpn[:, :]), sem=SC, waits=((DV, e1),))
                      ctx.ins("vector", lambda Ev: Ev.tensor_copy(dumm[:, :], dumm[:, :]))
                      e3 = ctx.ins("vector", lambda Ev: Ev.tensor_tensor(
                          atlb[:, :], tmpn[:, :], athb[:, :], ALU.subtract),
                          sem=DV, waits=((SC, e2),))
                      nbc_war[spar] = e1
                      pv_free = e1
                      ctx.dma("gpsimd", attnSh[h * 128:(h + 1) * 128, J * SPAN:(J + 1) * SPAN],
                              athb[:, :], ST, waits=((DV, e3),))
                      at_st = ctx.dma("gpsimd", attnSl[h * 128:(h + 1) * 128, J * SPAN:(J + 1) * SPAN],
                                      atlb[:, :], ST)

                  bqueue.append((tag, chainA))
                  bqueue.append((tag, chainB))
                  bqueue.append((tag, evac))
              hd_free[par] = PE.n
          pck, ppb, psl, pnk = prev_hi[0]
          ctx.ins("vector", lambda Ev: Ev.tensor_copy(dumm[:, :], dumm[:, :]))
          lo_val[pck] = ctx.ins("vector", lambda Ev, ppb=ppb, psl=psl, pnk=pnk: Ev.tensor_tensor(
              PloT[:, psl, 0:pnk], Pb[:, ppb, 0:pnk],
              PhiT[:, psl, 0:pnk], ALU.subtract), sem=DV)
          drain_thru(10 ** 9)
          p2_store_total = ST.n
          ctx.barrier()
      if "attn" in debug_outputs:
          pass
      if phases <= 2:
          raise _StopBuild

      with ExitStack() as p3:
          wstg = sb("wstg", [128, 2, CH, 128], p3)
          wohi = sb("wohi", [128, 8, CH, 128], p3, BF16)
          wolo = sb("wolo", [128, 8, CH, 128], p3, BF16)
          ash = sb("ash", [128, 2, CH, SPAN], p3, BF16)
          asl = sb("asl", [128, 2, CH, SPAN], p3, BF16)
          xres = sb("xres", [128, 2, SPAN], p3)
          stage3 = sb("stage3", [128, 2, SPAN], p3)
          pp3 = psum("pp3", [128, 2, SPAN], p3)
          wsplit_done = [0, 0]
          sp_free = [0, 0]
          x_free = [0, 0]
          st3_free = [0, 0]
          bank_free = [0, 0]

          def wo_load(half, oc8, pe_gate):
              oc = half * 8 + oc8
              par = oc8 % 2
              return ctx.dma("sync", wstg[:, par, :, :],
                             woT[:, oc * 128:(oc + 1) * 128].rearrange("(c p) o -> p c o", p=128),
                             LD, waits=((DV, wsplit_done[par]), (PE, pe_gate)))

          def wo_split(oc8, wv):
              par = oc8 % 2
              ctx.ins("vector", lambda Ev, par=par, oc8=oc8: Ev.tensor_copy(
                  wohi[:, oc8, :, :], wstg[:, par, :, :]), waits=((LD, wv),))
              wsplit_done[par] = ctx.ins("vector", lambda Ev, par=par, oc8=oc8: Ev.tensor_tensor(
                  wolo[:, oc8, :, :], wstg[:, par, :, :], wohi[:, oc8, :, :],
                  ALU.subtract), sem=DV)

          for oc8 in range(CH // 2):
              wo_split(oc8, wo_load(0, oc8, 0))
          wo_pend = [None]
          for half in range(2):
              g_wsplit = wsplit_done[(CH // 2 - 1) % 2]
              for sp in range(NSPAN):
                  spar = sp % 2
                  av = ctx.dma("sync", ash[:, spar, :, :],
                               attnSh[:, sp * SPAN:(sp + 1) * SPAN].rearrange("(c p) t -> p c t", p=128),
                               LD, waits=((PE, sp_free[spar]), (ST, p2_store_total)))
                  av = ctx.dma("sync", asl[:, spar, :, :],
                               attnSl[:, sp * SPAN:(sp + 1) * SPAN].rearrange("(c p) t -> p c t", p=128),
                               LD)
                  for oc8 in range(CH // 2):
                      oc = half * 8 + oc8
                      xpar = oc8 % 2
                      xv = ctx.dma("sync", xres[:, xpar, :],
                                   xT[oc * 128:(oc + 1) * 128, sp * SPAN:(sp + 1) * SPAN],
                                   LD, waits=((DV, x_free[xpar]),))
                      bank = oc8 % 2
                      for c in range(CH):
                          for ph, (wt, at_) in enumerate(((wohi, ash), (wohi, asl), (wolo, ash))):
                              v = ctx.ins("tensor", lambda Et, oc8=oc8, c=c, spar=spar, bank=bank, wt=wt, at_=at_, first=(c == 0 and ph == 0), last=(c == CH - 1 and ph == 2): Et.matmul(
                                  pp3[:, bank, :], wt[:, oc8, c, :],
                                  at_[:, spar, c, :],
                                  start=first, stop=last),
                                  sem=(PE if (c == CH - 1 and ph == 2) else None),
                                  waits=((LD, av), (DV, max(g_wsplit, bank_free[bank]))))
                      mmv = v
                      if half == 0 and sp == NSPAN - 1:
                          if wo_pend[0] is not None:
                              wo_split(*wo_pend[0])
                          wo_pend[0] = (oc8, wo_load(1, oc8, mmv))
                          if oc8 == CH // 2 - 1:
                              wo_split(*wo_pend[0])
                              wo_pend[0] = None
                      if oc8 == CH // 2 - 1:
                          sp_free[spar] = mmv
                      bank_free[bank] = ctx.ins("vector", lambda Ev, xpar=xpar, bank=bank: Ev.tensor_tensor(
                          stage3[:, xpar, :], pp3[:, bank, :],
                          xres[:, xpar, :], ALU.add),
                          sem=DV, waits=((PE, mmv), (LD, xv), (ST, st3_free[xpar])))
                      x_free[xpar] = bank_free[bank]
                      st3_free[xpar] = ctx.dma(
                          "gpsimd", x1Ts[oc * 128:(oc + 1) * 128, sp * SPAN:(sp + 1) * SPAN],
                          stage3[:, xpar, :], ST, waits=((DV, bank_free[bank]),))
          p3_store_total = ST.n
          ctx.barrier()
      attn_stack.close()
      if phases <= 3:
          raise _StopBuild

      with ExitStack() as p4o:
          sel8 = sb("sel8", [8, 512], p4o)
          selE = sb("selE", [8, 512], p4o)
          ctx.ins("gpsimd", lambda Eg: Eg.iota(sel8[:, :], [[1, 8], [0, 64]], channel_multiplier=0,
                                               allow_small_or_imprecise_dtypes=True))
          ctx.ins("gpsimd", lambda Eg: Eg.iota(selE[:, :], [[0, 8], [0, 64]], channel_multiplier=1,
                                               allow_small_or_imprecise_dtypes=True), sem=GP)
          g_io4 = GP.n
          sel_done = ctx.ins("vector", lambda Ev: Ev.tensor_tensor(
              sel8[:, :], sel8[:, :], selE[:, :], ALU.is_equal), sem=DV,
              waits=((GP, g_io4),))
          aTb = sb("aTb", [128, CH, E * R], p4o)
          bSb = sb("bSb", [128, 4, D], p4o)
          wrb = sb("wrb", [128, CH, E], p4o)
          rows2 = sb("rows2", [1, 3, T], p4o)
          r2col = sb("r2col", [128, NQT], p4o)
          nwA = sb("nwA", [1, E * R], p4o)
          nwR = sb("nwR", [1, E], p4o)

          with ExitStack() as p4a:
              x1tmp = sb("x1tmp", [128, 2, T], p4a)
              xsq4 = sb("xsq4", [128, 2, T], p4a)
              st4 = psum("st4", [128, 8, SPAN], p4a)
              sq_done = [0, 0]
              last_s2 = 0
              x1_ld = []
              for c in range(CH):
                  par = c % 2
                  x1_ld.append(ctx.dma("sync", x1tmp[:, par, :], x1Ts[c * 128:(c + 1) * 128, :], LD,
                                       waits=((PE, sq_done[par]), (ST, p3_store_total))))
                  if c == 1:
                      wr_ld = ctx.dma("sync", wrb[:, :, :], wrT.rearrange("(c p) o -> p c o", p=128), LD)
                      br_ld = ctx.dma("sync", br_row[:, :], br, LD, waits=((DV, g_consts),))
                  if c == 2:
                      aT_ld = ctx.dma("sync", aTb[:, :, :], aT.rearrange("(c p) o -> p c o", p=128), LD)
                  if c == 6:
                      bS_ld = ctx.dma("sync", bSb[:, :, :], bS.rearrange("(c p) o -> p c o", p=128), LD)
                  for sp in range(NSPAN):
                      ctx.ins("tensor", lambda Et, c=c, sp=sp, par=par: Et.matmul(
                          st4[:, sp, :], onesD[:, :], x1tmp[:, par, sp * SPAN:(sp + 1) * SPAN],
                          start=(c == 0), stop=(c == CH - 1)),
                          waits=((LD, x1_ld[c]),))
                  sqv = ctx.ins("scalar", lambda Es, c=c, par=par: Es.activation(
                      xsq4[:, par, :], x1tmp[:, par, :], AF.Square),
                      sem=SC, waits=((LD, x1_ld[c]),))
                  for sp in range(NSPAN):
                      v = ctx.ins("tensor", lambda Et, par=par, sp=sp, c=c: Et.matmul(
                          st4[:, 4 + sp, :], onesD[:, :], xsq4[:, par, sp * SPAN:(sp + 1) * SPAN],
                          start=(c == 0), stop=(c == CH - 1)),
                          sem=(PE if sp == NSPAN - 1 else None), waits=((SC, sqv),))
                      if sp == NSPAN - 1:
                          sq_done[par] = v
                          last_s2 = v
              for sp in range(NSPAN):
                  ctx.ins("vector", lambda Ev, sp=sp: Ev.tensor_copy(
                      rows2[:, 0, sp * SPAN:(sp + 1) * SPAN], st4[0:1, sp, :]),
                      waits=((PE, last_s2),))
                  ctx.ins("vector", lambda Ev, sp=sp: Ev.tensor_mul(
                      rows2[:, 2, sp * SPAN:(sp + 1) * SPAN],
                      rows2[:, 0, sp * SPAN:(sp + 1) * SPAN],
                      rows2[:, 0, sp * SPAN:(sp + 1) * SPAN]))
                  ctx.ins("vector", lambda Ev, sp=sp: Ev.tensor_tensor(
                      rows2[:, 2, sp * SPAN:(sp + 1) * SPAN], st4[0:1, 4 + sp, :],
                      rows2[:, 2, sp * SPAN:(sp + 1) * SPAN], ALU.subtract), sem=DV)
              ctx.ins("vector", lambda Ev: Ev.tensor_scalar_add(rows2[:, 2, :], rows2[:, 2, :], EPS),
                      sem=DV)
              g_m2 = DV.n
              sqv = ctx.ins("scalar", lambda Es: Es.activation(
                  rows2[:, 2, :], rows2[:, 2, :], AF.Sqrt), sem=SC, waits=((DV, g_m2),))
              ctx.ins("vector", lambda Ev: Ev.reciprocal(rows2[:, 1, :], rows2[:, 2, :]),
                      sem=DV, waits=((SC, sqv),))
              g_r2 = DV.n
              cp = g_r2
              for i in range(NQT):
                  v = ctx.ins("tensor", lambda Et, i=i: Et.matmul(
                      st4[:, 0, 0:1], rows2[0:1, 1, i * 128:(i + 1) * 128], one1[:, :],
                      start=True, stop=True), sem=PE, waits=((DV, cp),))
                  cp = ctx.ins("vector", lambda Ev, i=i: Ev.tensor_copy(
                      r2col[:, i:i + 1], st4[:, 0, 0:1]), sem=DV, waits=((PE, v),))
              v = ctx.ins("tensor", lambda Et: Et.matmul(
                  st4[:, 1, 0:E], onec[:, :], br_row[:, :], start=True, stop=True),
                  sem=PE, waits=((LD, br_ld), (DV, g_m2)))
              ctx.ins("vector", lambda Ev: Ev.tensor_copy(br_bc[:, :], st4[:, 1, 0:E]),
                      sem=DV, waits=((PE, v),))
              for c in range(CH):
                  v = ctx.ins("tensor", lambda Et, c=c: Et.matmul(
                      st4[:, 2, 0:E * R], ones1[:, :], aTb[:, c, :],
                      start=(c == 0), stop=(c == CH - 1)),
                      sem=(PE if c == CH - 1 else None), waits=((LD, aT_ld), (DV, g_m2)))
              ctx.ins("vector", lambda Ev: Ev.tensor_scalar_mul(
                  nwA[:, :], st4[0:1, 2, 0:E * R], -1.0), sem=DV, waits=((PE, v),))
              for c in range(CH):
                  v = ctx.ins("tensor", lambda Et, c=c: Et.matmul(
                      st4[:, 3, 0:E], ones1[:, :], wrb[:, c, :],
                      start=(c == 0), stop=(c == CH - 1)),
                      sem=(PE if c == CH - 1 else None), waits=((LD, wr_ld), (DV, g_m2)))
              ctx.ins("vector", lambda Ev: Ev.tensor_scalar_mul(
                  nwR[:, :], st4[0:1, 3, 0:E], -1.0), sem=DV, waits=((PE, v),))
              g_rows = DV.n
              ctx.barrier()

          with ExitStack() as p4b:
              x1c = sb("x1c", [128, 2, CH, SPAN], p4b)
              hb = sb("hb", [128, 2, 4, SPAN], p4b)
              lgt = sb("lgt", [128, 2, E], p4b)
              sc_tok = sb("sc_tok", [128, 2, E], p4b)
              scT = sb("scT", [8, 2, SPAN], p4b)
              scbc = sb("scbc", [128, 2, 4, SPAN], p4b)
              tmax = sb("tmax", [128, 1], p4b)
              tsum = sb("tsum", [128, 1], p4b)
              tw = sb("tw", [128, 1], p4b)
              twr = sb("twr", [128, 1], p4b)
              vdm = sb("vdm", [128, 1], p4b)
              vdm2 = sb("vdm2", [128, 1], p4b)
              dbgA = sb("dbgA", [128, 8], p4b)
              dbgB = sb("dbgB", [128, 8], p4b)
              lgsb = sb("lgsb", [128, 8], p4b)
              ost = sb("ost", [128, 2, SPAN], p4b)
              for tz in (tmax, tsum, tw, twr, vdm, vdm2):
                  ctx.ins("vector", lambda Ev, tz=tz: Ev.memset(tz[:, :], 0.0))
              ctx.ins("vector", lambda Ev: Ev.memset(lgt[:, :, :], 0.0))
              ctx.ins("vector", lambda Ev: Ev.memset(sc_tok[:, :, :], 0.0))
              lg_ps = psum("lg_ps", [128, 2, SPAN], p4b)
              tp_ps = psum("tp_ps", [128, SPAN], p4b)
              h_ps = psum("h_ps", [128, 2, SPAN], p4b)
              o_ps = psum("o_ps", [128, 2, SPAN], p4b)

              x1_pe_free = [0, 0]
              x1_dv_free = [0, 0]
              h_free = [0, 0]
              hb_rdy = [0, 0]
              ost_free = [0, 0]
              obank_free = [0, 0]
              hbank_free = [0, 0]
              lg_rel = [0, 0]
              scT_free = [0, 0]
              scv_rel = [0, 0]
              lgtp_rel = 0
              sctp_rel = 0
              for sp in range(NSPAN):
                  par = sp % 2
                  xv = ctx.dma("sync", x1c[:, par, :, :],
                               x1Ts[:, sp * SPAN:(sp + 1) * SPAN].rearrange("(c p) t -> p c t", p=128),
                               LD, waits=((PE, x1_pe_free[par]), (DV, x1_dv_free[par]),
                                          (ST, p3_store_total)))
                  for ti in range(4):
                      tt = sp * 4 + ti
                      tslot = ti % 2
                      for c in range(CH):
                          ctx.ins("tensor", lambda Et, c=c, par=par, ti=ti, tslot=tslot: Et.matmul(
                              lg_ps[:, tslot, 0:E], x1c[:, par, c, ti * 128:(ti + 1) * 128],
                              wrb[:, c, :], start=(c == 0), stop=False),
                              waits=((LD, max(xv, wr_ld)),
                                     (DV, max(lg_rel[tslot], g_rows))))
                      v = ctx.ins("tensor", lambda Et, tt=tt, tslot=tslot: Et.matmul(
                          lg_ps[:, tslot, 0:E], rows2[0:1, 0, tt * 128:(tt + 1) * 128],
                          nwR[:, :], start=False, stop=True), sem=PE)
                      lg_rel[tslot] = ctx.ins("vector", lambda Ev, tt=tt, tslot=tslot: Ev.scalar_tensor_tensor(
                          lgt[:, tslot, :], lg_ps[:, tslot, 0:E], r2col[:, tt:tt + 1],
                          br_bc[:, :], ALU.mult, ALU.add), sem=DV,
                          waits=((PE, v), (SC, scv_rel[tslot])))
                      if "p4dbg" in debug_outputs and sp == 0 and ti == 0:
                          ctx.ins("vector", lambda Ev, tslot=tslot: Ev.tensor_copy(
                              dbgA[:, :], lgt[:, tslot, :]))
                      ctx.ins("vector", lambda Ev: Ev.tensor_copy(vdm[:, :], vdm2[:, :]))
                      ctx.ins("vector", lambda Ev, tslot=tslot: Ev.reduce_max(
                          tmax[:, :], lgt[:, tslot, :], axis=AX.X))
                      ctx.ins("vector", lambda Ev: Ev.tensor_copy(vdm[:, :], vdm2[:, :]))
                      ctx.ins("vector", lambda Ev, tslot=tslot: Ev.tensor_scalar(
                          sc_tok[:, tslot, :], lgt[:, tslot, :], tmax[:, :], None, ALU.is_equal))
                      lsub = ctx.ins("vector", lambda Ev, tslot=tslot: Ev.tensor_scalar(
                          lgt[:, tslot, :], lgt[:, tslot, :], tmax[:, :], None, ALU.subtract),
                          sem=DV)
                      scv_rel[tslot] = ctx.ins("scalar", lambda Es, tslot=tslot: Es.activation(
                          lgt[:, tslot, :], lgt[:, tslot, :], AF.Exp,
                          accum_out=tsum[:, :]), sem=SC, waits=((DV, lsub),))
                      ctx.ins("vector", lambda Ev: Ev.reciprocal(tw[:, :], tsum[:, :]),
                              waits=((SC, scv_rel[tslot]),))
                      ctx.ins("vector", lambda Ev, tt=tt: Ev.tensor_scalar(
                          twr[:, :], tw[:, :], r2col[:, tt:tt + 1], INV_RANK,
                          ALU.mult, ALU.mult))
                      ctx.ins("vector", lambda Ev: Ev.tensor_copy(vdm[:, :], vdm2[:, :]))
                      dvv = ctx.ins("vector", lambda Ev, tslot=tslot: Ev.tensor_scalar(
                          sc_tok[:, tslot, :], sc_tok[:, tslot, :], twr[:, :], None, ALU.mult),
                          sem=DV)
                      if "p4dbg" in debug_outputs and sp == 0 and ti == 0:
                          ctx.dma("gpsimd", dbgC[:, 0:8], sc_tok[:, tslot, :], ST,
                                  waits=((DV, dvv),), slow=True)
                          ctx.dma("gpsimd", dbgC[:, 8:9], tmax[:, :], ST, slow=True)
                          ctx.dma("gpsimd", dbgC[:, 9:10], twr[:, :], ST, slow=True)
                          ctx.dma("gpsimd", dbgC[:, 10:11], tw[:, :], ST, slow=True)
                          ctx.dma("gpsimd", dbgC[:, 11:12], r2col[:, 0:1], ST, slow=True)
                          ctx.dma("gpsimd", dbgP[0:128, 1024:1032], dbgA[:, :], ST,
                                  waits=((DV, lsub),))
                          ctx.dma("gpsimd", dbgP[0:128, 1056:1064], dbgB[:, :], ST,
                                  waits=((DV, lsub),))
                          ctx.dma("gpsimd", dbgP[0:128, 1040:1048], sc_tok[:, tslot, :], ST,
                                  waits=((DV, dvv),))
                      v = ctx.ins("tensor", lambda Et, tslot=tslot, ti=ti: Et.transpose(
                          tp_ps[0:8, ti * 128:(ti + 1) * 128], sc_tok[:, tslot, :], ident[:, :]),
                          sem=PE, waits=((SC, sctp_rel), (DV, dvv)))
                      sctp_rel = ctx.ins("scalar", lambda Es, par=par, ti=ti: Es.copy(
                          scT[0:8, par, ti * 128:(ti + 1) * 128],
                          tp_ps[0:8, ti * 128:(ti + 1) * 128]),
                          sem=SC, waits=((PE, max(v, scT_free[par])),))
                  scT_rdy = SC.n
                  if "p4dbg" in debug_outputs and sp == 0:
                      ctx.dma("gpsimd", dbgP[0:8, 0:SPAN], scT[0:8, par, :], ST,
                              waits=((SC, scT_rdy),))
                  for ec in range(4):
                      bank = ec % 2
                      v = ctx.ins("tensor", lambda Et, ec=ec, par=par, bank=bank: Et.matmul(
                          h_ps[:, bank, :], sel8[:, ec * 128:(ec + 1) * 128], scT[:, par, :],
                          start=True, stop=True), sem=PE,
                          waits=((SC, max(scT_rdy, hbank_free[bank])), (DV, max(hb_rdy[bank], sel_done))))
                      scT_free[par] = v
                      hbank_free[bank] = ctx.ins("scalar", lambda Es, ec=ec, par=par, bank=bank: Es.copy(
                          scbc[:, par, ec, :], h_ps[:, bank, :]), sem=SC, waits=((PE, v),))
                  if "p4dbg" in debug_outputs and sp == 0:
                      ctx.dma("gpsimd", dbgP[128:256, 0:SPAN], scbc[:, par, 0, :], ST,
                              waits=((SC, hbank_free[0]),))
                  for ec in range(4):
                      bank = ec % 2
                      for c in range(CH):
                          ctx.ins("tensor", lambda Et, c=c, ec=ec, par=par, bank=bank: Et.matmul(
                              h_ps[:, bank, :], aTb[:, c, ec * 128:(ec + 1) * 128],
                              x1c[:, par, c, :], start=(c == 0), stop=False),
                              waits=((LD, max(xv, aT_ld)), (SC, hbank_free[bank]),
                                     (DV, hb_rdy[bank])))
                      v = ctx.ins("tensor", lambda Et, ec=ec, sp=sp, bank=bank: Et.matmul(
                          h_ps[:, bank, :], nwA[:, ec * 128:(ec + 1) * 128],
                          rows2[0:1, 0, sp * SPAN:(sp + 1) * SPAN], start=False, stop=True),
                          sem=PE)
                      hb_rdy[bank] = ctx.ins("vector", lambda Ev, ec=ec, par=par, bank=bank: Ev.tensor_tensor(
                          hb[:, par, ec, :], h_ps[:, bank, :], scbc[:, par, ec, :], ALU.mult),
                          sem=DV, waits=((PE, max(v, h_free[par])),))
                  h_rdy = DV.n
                  if "p4dbg" in debug_outputs and sp == 0:
                      ctx.dma("gpsimd", dbgP[256:384, 0:SPAN], hb[:, par, 0, :], ST,
                              waits=((DV, h_rdy),))
                  for oc in range(CH):
                      bank = oc % 2
                      opar = oc % 2
                      for ec in range(4):
                          v = ctx.ins("tensor", lambda Et, ec=ec, oc=oc, par=par, bank=bank: Et.matmul(
                              o_ps[:, bank, :], bSb[:, ec, oc * 128:(oc + 1) * 128],
                              hb[:, par, ec, :], start=(ec == 0), stop=(ec == 3)),
                              sem=(PE if ec == 3 else None),
                              waits=((DV, max(h_rdy, obank_free[bank])), (LD, bS_ld)))
                      mmv = v
                      if oc == CH - 1:
                          h_free[par] = mmv
                      obank_free[bank] = ctx.ins("vector", lambda Ev, oc=oc, par=par, bank=bank, opar=opar: Ev.tensor_tensor(
                          ost[:, opar, :], o_ps[:, bank, :], x1c[:, par, oc, :], ALU.add),
                          sem=DV, waits=((PE, mmv), (ST, ost_free[opar])))
                      ost_free[opar] = ctx.dma(
                          "gpsimd", outT[oc * 128:(oc + 1) * 128, sp * SPAN:(sp + 1) * SPAN],
                          ost[:, opar, :], ST, waits=((DV, obank_free[bank]),))
                  x1_pe_free[par] = PE.n
                  x1_dv_free[par] = DV.n

              ctx.wait("gpsimd", ST, ST.n)
    except _StopBuild:
        attn_stack.close()
        ctx.wait("gpsimd", ST, ST.n)

    with nc.Block() as block:
        for eng in ENGINES:
            def body(Eng, eng=eng):
                for fn in ctx.streams[eng]:
                    fn(Eng)
            getattr(block, eng)(body)
    es.close()
    return nc


def _prep_inputs(L0_attn_x, L0_attn_Wq, L0_attn_Wk, L0_attn_Wv, L0_attn_Wo,
                 L0_ffn_A, L0_ffn_B, L0_ffn_Wr, L0_ffn_br):
    C = np.ascontiguousarray
    f = np.float32
    wqT = C(np.asarray(L0_attn_Wq, dtype=f).T)
    wkT = C(np.asarray(L0_attn_Wk, dtype=f).T)
    wvT = C(np.asarray(L0_attn_Wv, dtype=f).T)
    woT = C(np.asarray(L0_attn_Wo, dtype=f).T)
    aT = C(np.asarray(L0_ffn_A, dtype=f).reshape(E * R, D).T)
    bSm = C(np.asarray(L0_ffn_B, dtype=f).transpose(0, 2, 1).reshape(E * R, D))
    wrT = C(np.asarray(L0_ffn_Wr, dtype=f).T)
    brm = C(np.asarray(L0_ffn_br, dtype=f).reshape(1, E))
    x = np.asarray(L0_attn_x, dtype=f)
    in_maps = []
    for b in range(8):
        in_maps.append({
            "xT": C(x[b].T),
            "wqT": wqT, "wkT": wkT, "wvT": wvT, "woT": woT,
            "aT": aT, "bS": bSm, "wrT": wrT, "br": brm,
        })
    return in_maps


def kernel(L0_attn_x, L0_attn_Wq, L0_attn_Wk, L0_attn_Wv, L0_attn_Wo,
           L0_ffn_A, L0_ffn_B, L0_ffn_Wr, L0_ffn_br):
    in_maps = _prep_inputs(L0_attn_x, L0_attn_Wq, L0_attn_Wk, L0_attn_Wv, L0_attn_Wo,
                           L0_ffn_A, L0_ffn_B, L0_ffn_Wr, L0_ffn_br)
    nc = build_nc()
    res = run_bass_kernel_spmd(nc, in_maps, core_ids=list(range(8)))
    out = np.stack([np.asarray(r["outT"], dtype=np.float32).T for r in res.results], axis=0)
    return np.ascontiguousarray(out)
